# version 13
# baseline (speedup 1.0000x reference)
import numpy as np
import concourse.bass as bass
import concourse.mybir as mybir
from concourse.bass_utils import run_bass_kernel_spmd
from contextlib import ExitStack

F32 = mybir.dt.float32
BF16 = mybir.dt.bfloat16
AF = mybir.ActivationFunctionType
ALU = mybir.AluOpType
AX = mybir.AxisListType

D = 1024
I = 512
E = 32
NEXP = E + 1
NG, EPG, TOPG, TOPK = 8, 4, 4, 8
WARM = 128
OWN = 4096
SLICE = WARM + OWN
KC = 8
IC = 4
EPS = 1e-5
BIGI = 1.0e4

TILES_A = [1024, 1024, 1024, 1024, 128]
TILES_B = [1024, 1024, 1024, 1024, 128]
TILES_C = [1024, 1024, 1024, 1024]
TMAX = 1024
CHMAX = TMAX // 128
YBANKS = [4, 5, 7]


def _subs(T):
    out, o = [], 0
    while o < T:
        w = min(512, T - o)
        out.append((o, w))
        o += w
    return out


class Sem:
    def __init__(self, h):
        self.h = h
        self.n = 0

    def bump(self, k=1):
        self.n += k
        return self.n


class EQ:
    def __init__(self):
        self.ops = []
        self.hw = {}

    def op(self, fn):
        self.ops.append(fn)

    def wait(self, sem, val):
        if val is None or val <= 0:
            return
        assert val <= sem.n, "wait on future event (plan bug)"
        if self.hw.get(id(sem), -1) >= val:
            return
        self.hw[id(sem)] = val
        self.op(lambda e, s=sem.h, v=val: e.wait_ge(s, v))

    def waits(self, evts):
        for sem, val in evts:
            self.wait(sem, val)


class Builder:
    def __init__(self, nc):
        self.nc = nc
        self.q = {k: EQ() for k in ("sync", "gpsimd", "tensor", "vector", "scalar")}
        self.dirty = {"vector": set(), "scalar": set(), "gpsimd": set()}
        self.bank = {i: [] for i in range(8)}

    def dma(self, queue, out, in_, sem, waits=()):
        self.q[queue].waits(waits)
        self.q[queue].op(lambda e, o=out, i=in_, s=sem.h:
                         e.dma_start(out=o, in_=i).then_inc(s, 16))
        return sem.bump(16)

    def mm(self, out, lhsT, rhs, start, stop, sem=None):
        if sem is None:
            self.q["tensor"].op(lambda e, o=out, l=lhsT, r=rhs, a=start, b=stop:
                                e.matmul(o, l, r, start=a, stop=b))
            return None
        self.q["tensor"].op(lambda e, o=out, l=lhsT, r=rhs, a=start, b=stop, s=sem.h:
                            e.matmul(o, l, r, start=a, stop=b).then_inc(s, 1))
        return sem.bump()

    def bank_acquire(self, i):
        self.q["tensor"].waits(self.bank[i])
        self.bank[i] = []

    def bank_consumed(self, i, sem, evt):
        self.bank[i].append((sem, evt))

    def vop(self, fn, reads=(), writes=(), sem=None, eng="vector"):
        d = self.dirty[eng]
        if any(r in d for r in reads):
            self.q[eng].op(lambda e: e.drain())
            d.clear()
        for w in writes:
            d.add(w)
        if sem is None:
            self.q[eng].op(fn)
            return None
        self.q[eng].op(lambda e, f=fn, s=sem.h: f(e).then_inc(s, 1))
        return sem.bump()


def build_program():
    nc = bass.Bass("TRN2", target_bir_lowering=False)
    st = ExitStack()

    x_h = nc.dram_tensor("x", [D, SLICE], F32, kind="ExternalInput").ap()
    wg_h = nc.dram_tensor("wg", [E + 1, D, I], F32, kind="ExternalInput").ap()
    wu_h = nc.dram_tensor("wu", [E + 1, D, I], F32, kind="ExternalInput").ap()
    wd_h = nc.dram_tensor("wd", [E + 1, I, D], F32, kind="ExternalInput").ap()
    gw_h = nc.dram_tensor("gw", [D, E], F32, kind="ExternalInput").ap()
    win_h = nc.dram_tensor("win", [D, D], F32, kind="ExternalInput").ap()
    wr_h = nc.dram_tensor("wr", [D, D], F32, kind="ExternalInput").ap()
    wi_h = nc.dram_tensor("wi", [D, D], F32, kind="ExternalInput").ap()
    wo_h = nc.dram_tensor("wo", [D, D], F32, kind="ExternalInput").ap()
    bin_h = nc.dram_tensor("bin", [128, KC], F32, kind="ExternalInput").ap()
    br_h = nc.dram_tensor("br", [128, KC], F32, kind="ExternalInput").ap()
    bi_h = nc.dram_tensor("bi", [128, KC], F32, kind="ExternalInput").ap()
    bo_h = nc.dram_tensor("bo", [128, KC], F32, kind="ExternalInput").ap()
    la_h = nc.dram_tensor("la", [128, KC], F32, kind="ExternalInput").ap()
    io_h = nc.dram_tensor("iota", [128, E], F32, kind="ExternalInput").ap()
    id_h = nc.dram_tensor("ident", [128, 128], F32, kind="ExternalInput").ap()
    mk_h = nc.dram_tensor("mask", [128, 1], F32, kind="ExternalInput").ap()
    out_h = nc.dram_tensor("out", [D, OWN], F32, kind="ExternalOutput").ap()
    x1_h = nc.dram_tensor("x1scr", [D, SLICE], F32, kind="Internal").ap()
    x2_h = nc.dram_tensor("x2scr", [D, OWN], F32, kind="Internal").ap()
    wgs_h = nc.dram_tensor("wgscr", [E + 1, D, I], BF16, kind="Internal").ap()
    wus_h = nc.dram_tensor("wuscr", [E + 1, D, I], BF16, kind="Internal").ap()
    wds_h = nc.dram_tensor("wdscr", [E + 1, I, D], BF16, kind="Internal").ap()

    POOL = 206 * 1024
    pool_h = st.enter_context(nc.sbuf_tensor("pool", [128, POOL], mybir.dt.uint8))
    pool_addr = nc.lookup_mloc(pool_h).addr
    cur = [0]

    def carve(name, shape, dtype, at=None):
        per = int(np.prod(shape[1:])) * mybir.dt.size(dtype)
        if at is None:
            at = cur[0]
            cur[0] = (at + per + 31) & ~31
        assert at + per <= POOL, f"SBUF pool overflow at {name}: {at + per}"
        hnd = nc.virtual_sbuf_tensor(name, shape, dtype)
        nc._tensor(hnd.name, hnd.shape, hnd.dtype, type="SB")
        mloc = nc.lookup_mloc(hnd)
        mloc.allocated = True
        mloc.addr = pool_addr + at
        return hnd.ap()

    ident = carve("ident", [128, 128], F32)
    iota = carve("iota", [128, E], F32)
    onesc = carve("onesc", [128, 1], F32)
    onesr = carve("onesr", [1, 128], F32)
    onesc_b = carve("onesc_b", [128, 1], BF16)
    onesr_b = carve("onesr_b", [1, 128], BF16)
    mrow_b = carve("mrow_b", [1, 512], BF16)
    binT = carve("binT", [128, KC], F32)
    brT = carve("brT", [128, KC], F32)
    biT = carve("biT", [128, KC], F32)
    boT = carve("boT", [128, KC], F32)
    laT = carve("laT", [128, KC], F32)
    avec = carve("avec", [128, KC], F32)
    maskc = carve("maskc", [128, 1], F32)
    gwT = carve("gwT", [128, KC * E], BF16)
    hcar = carve("hcar", [128, KC], F32)
    nbuf = carve("nbuf", [128, 2 * KC * TMAX], BF16)
    acc = carve("acc", [128, KC * TMAX], F32)
    xt = carve("xt", [128, KC * 512], F32)
    sqs = carve("sqs", [128, 2 * 512], BF16)
    xb_all = carve("xb_all", [128, KC * 512], BF16)
    mrow = carve("mrow", [1, 512], F32)
    vrow = carve("vrow", [1, 512], F32)
    srow = carve("srow", [1, 512], F32)
    gev = carve("gev", [128, 2 * 512], F32)
    ovl0 = cur[0]

    hslots = carve("hslots", [128, 2 * IC * 512], BF16)
    wgbuf = carve("wgbuf", [128, 2 * KC * I], BF16)
    wubuf = carve("wubuf", [128, 2 * KC * I], BF16)
    wdbuf = carve("wdbuf", [128, 3 * IC * D], BF16)
    gs_scr = carve("gs_scr", [128, IC * 512], F32)
    uc_scr = carve("uc_scr", [128, IC * 512], BF16)
    c_sb = carve("c_sb", [128, 2 * 512], F32)
    cst = carve("cst", [1, 2 * TMAX], F32)
    ffn_low = cur[0]

    cur[0] = ovl0
    bbuf = carve("bbuf", [128, KC * TMAX], F32)
    hb16 = carve("hb16", [128, KC * TMAX], BF16)
    hscr = carve("hscr", [128, TMAX], F32)
    igs = carve("igs", [128, 512], F32)
    ups = carve("ups", [128, 512], F32)
    wrbuf = carve("wrbuf", [128, 2 * KC * 128], BF16)
    wibuf = carve("wibuf", [128, 2 * KC * 128], BF16)
    wnbuf = carve("wnbuf", [128, 2 * KC * 128], BF16)
    wobuf = carve("wobuf", [128, KC * KC * 128], BF16)
    grf_end = cur[0]

    cur[0] = max(ffn_low, grf_end)
    cex = carve("cex", [32, 2 * TMAX], F32)
    sall = carve("sall", [128, CHMAX * E], F32)
    r_ms = carve("r_ms", [128, CHMAX * E], F32)
    r_t1 = carve("r_t1", [128, CHMAX * E], F32)
    r_t2 = carve("r_t2", [128, CHMAX * E], F32)
    r_gs = carve("r_gs", [128, CHMAX * NG], F32)
    r_gm = carve("r_gm", [128, CHMAX * NG], F32)
    r_mx = carve("r_mx", [128, CHMAX], F32)
    r_cb = carve("r_cb", [128, CHMAX * E], F32)
    r_ws = carve("r_ws", [128, CHMAX], F32)
    assert cur[0] <= POOL, f"pool overflow: {cur[0]}"

    PB = [st.enter_context(nc.psum_tensor(f"pb{i}", [128, 512], F32)).ap()
          for i in range(8)]

    sx = Sem(st.enter_context(nc.semaphore("sx")))
    gq = Sem(st.enter_context(nc.semaphore("gq")))
    pe = Sem(st.enter_context(nc.semaphore("pe")))
    ve = Sem(st.enter_context(nc.semaphore("ve")))
    se = Sem(st.enter_context(nc.semaphore("se")))
    po = Sem(st.enter_context(nc.semaphore("po")))

    B = Builder(nc)
    qT, qV, qS, qD, qG = (B.q[k] for k in ("tensor", "vector", "scalar", "sync", "gpsimd"))

    nv2 = nbuf.rearrange("p (b k t) -> p b k t", b=2, k=KC)
    av = acc.rearrange("p (k t) -> p k t", k=KC)
    bv = bbuf.rearrange("p (k t) -> p k t", k=KC)
    hv = hb16.rearrange("p (k t) -> p k t", k=KC)
    xv = xt.rearrange("p (k t) -> p k t", k=KC)
    wg_v = wgbuf.rearrange("p (b k n) -> p b k n", b=2, k=KC)
    wu_v = wubuf.rearrange("p (b k n) -> p b k n", b=2, k=KC)
    wd_v = wdbuf.rearrange("p (b k n) -> p b k n", b=3, k=IC)
    hs_v = hslots.rearrange("p (s k n) -> p s k n", s=2, k=IC)
    wr_v = wrbuf.rearrange("p (b k n) -> p b k n", b=2, k=KC)
    wi_v = wibuf.rearrange("p (b k n) -> p b k n", b=2, k=KC)
    wn_v = wnbuf.rearrange("p (b k n) -> p b k n", b=2, k=KC)
    wo_v = wobuf.rearrange("p (m k n) -> p m k n", m=KC, k=KC)
    cex2 = cex.rearrange("q (b t) -> q b t", b=2)

    HZ = {
        "xt_ve": 0, "xt_sx": 0,
        "sq_pe": {},
        "gev_ve": [0, 0],
        "csb_ve": [0, 0],
        "cst_pe": [0, 0],
        "cex_sx": [0, 0],
        "route_ve": 0,
        "n_pe": [0, 0],
        "e_cex": [0, 0],
        "n_ve": [0, 0],
        "igs_ve": 0, "ups_ve": 0,
        "hb_pe": 0,
        "ms_pe": 0,
        "row_po": 0,
        "row_ve": 0,
        "e_cb": 0,
        "gs_ve": [0, 0, 0, 0],
        "wgbuf_pe": {}, "wdbuf_pe": {},
        "hslot_pe": {},
        "ln_banks": (0, 1),
        "bf16w": False,
        "e_cast": {},
        "xb_pe": 0,
    }

    e_const = 0
    for t, h in [(ident, id_h), (iota, io_h), (binT, bin_h), (brT, br_h),
                 (biT, bi_h), (boT, bo_h), (laT, la_h), (maskc, mk_h)]:
        e_const = B.dma("sync", t, h, sx)
    e_gwT = B.dma("gpsimd", gwT.rearrange("p (k e) -> p k e", k=KC),
                  gw_h.rearrange("(k p) e -> p k e", p=128), gq)
    qV.op(lambda e: e.memset(onesc, 1.0))
    qV.op(lambda e: e.memset(onesr, 1.0))
    qV.op(lambda e: e.memset(onesc_b, 1.0))
    qV.op(lambda e: e.memset(onesr_b, 1.0))
    e_hc0 = B.vop(lambda e: e.memset(hcar, 0.0), writes=("hcar",), sem=ve)
    qS.wait(sx, e_const)
    e_avec = B.vop(lambda e: e.activation(avec, laT, AF.Sigmoid),
                   writes=("avec",), sem=se, eng="scalar")

    def layer_norm(src_dram, t0, T, par, ready_evt):
        nv = nv2[:, par]
        for so, w in _subs(T):
            ln_load(src_dram, t0, so, w, ready_evt)
            ln_sub(par, nv, so, w)

    def ln_load(src_dram, t0, so, w, ready_evt):
        HZ["ln_ld"] = B.dma(
            "sync", xv[:, :, 0:w],
            src_dram[:, t0 + so:t0 + so + w].rearrange("(k p) t -> p k t", p=128),
            sx, waits=[(ve, HZ["xt_ve"]), (sx, max(HZ["xt_sx"], ready_evt))])
        HZ["ln_xb"] = B.dma(
            "gpsimd", xb_all.rearrange("p (k t) -> p k t", k=KC)[:, :, 0:w],
            src_dram[:, t0 + so:t0 + so + w].rearrange("(k p) t -> p k t", p=128),
            gq, waits=[(pe, HZ["xb_pe"])])

    def ln_sub(par, nv, so, w):
        if True:
            e_ld = HZ["ln_ld"]
            qT.wait(sx, e_ld)
            qS.wait(sx, e_ld)
            bA, bB = HZ["ln_banks"]
            B.bank_acquire(bA)
            B.bank_acquire(bB)
            pm = PB[bA][0:1, 0:w]
            pv = PB[bB][0:1, 0:w]
            e_xb = HZ["ln_xb"]
            qT.wait(gq, e_xb)
            xbv = xb_all.rearrange("p (k t) -> p k t", k=KC)
            B.q["gpsimd"].wait(gq, e_xb)
            for k in range(KC):
                slot = k % 2
                sq = sqs[:, slot * 512:slot * 512 + w]
                B.q["gpsimd"].wait(pe, HZ["sq_pe"].get(slot, 0))
                e_sq = B.vop(lambda e, o=sq, a=xbv[:, k, 0:w], b=xbv[:, k, 0:w]:
                             e.tensor_tensor(o, a, b, ALU.mult),
                             writes=("sqs",), sem=po, eng="gpsimd")
                qT.wait(po, e_sq)
                B.mm(pm, onesc_b, xbv[:, k, 0:w], start=(k == 0), stop=(k == KC - 1),
                     sem=pe if k == KC - 1 else None)
                ev = B.mm(pv, onesc_b, sq, start=(k == 0), stop=(k == KC - 1), sem=pe)
                HZ["sq_pe"][slot] = ev
            e_pm = pe.n
            e_pv = pe.n
            HZ["xb_pe"] = e_pm
            qS.wait(pe, e_pm)
            qS.wait(pe, HZ["ms_pe"])
            qS.wait(po, HZ["row_po"])
            qS.wait(ve, HZ["row_ve"])
            e_m = B.vop(lambda e, o=mrow[:, 0:w], i=pm:
                        e.activation(o, i, AF.Identity, scale=1.0 / D),
                        writes=("mrow",), sem=se, eng="scalar")
            qS.wait(pe, e_pv)
            e_v0 = B.vop(lambda e, o=vrow[:, 0:w], i=pv:
                         e.activation(o, i, AF.Identity, scale=1.0 / D),
                         writes=("vrow",), sem=se, eng="scalar")
            B.bank_consumed(bA, se, e_m)
            B.bank_consumed(bB, se, e_v0)
            B.vop(lambda e, o=mrow_b[:, 0:w], i=mrow[:, 0:w]:
                  e.activation(o, i, AF.Identity),
                  reads=("mrow",), writes=("mrow_b",), sem=None, eng="scalar")
            e_msq = B.vop(lambda e, o=srow[:, 0:w], i=mrow[:, 0:w]:
                          e.activation(o, i, AF.Square),
                          reads=("mrow",), writes=("srow",), sem=se, eng="scalar")
            B.q["gpsimd"].wait(se, e_msq)
            B.vop(lambda e, o=vrow[:, 0:w], a=vrow[:, 0:w], b=srow[:, 0:w]:
                  e.tensor_tensor(o, a, b, ALU.subtract),
                  writes=("vrow",), eng="gpsimd")
            e_var = B.vop(lambda e, o=vrow[:, 0:w], i=vrow[:, 0:w]:
                          e.tensor_scalar(o, i, EPS, None, ALU.add),
                          reads=("vrow",), writes=("vrow",), sem=po, eng="gpsimd")
            HZ["row_po"] = e_var
            qS.wait(po, e_var)
            e_sqr = B.vop(lambda e, o=srow[:, 0:w], i=vrow[:, 0:w]:
                          e.activation(o, i, AF.Sqrt),
                          reads=("srow",), writes=("srow",), sem=se, eng="scalar")
            qV.wait(se, e_sqr)
            e_sr = B.vop(lambda e, o=srow[:, 0:w], i=srow[:, 0:w]: e.reciprocal(o, i),
                         reads=("srow",), writes=("srow",), sem=ve)
            HZ["row_ve"] = e_sr
            qT.wait(ve, e_sr)
            B.bank_acquire(bA)
            B.bank_acquire(bB)
            Mb = PB[bA][:, 0:w]
            Sb = PB[bB][:, 0:w]
            B.mm(Mb, onesr_b, mrow_b[0:1, 0:w], start=True, stop=True)
            B.mm(Sb, onesr, srow[0:1, 0:w], start=True, stop=True, sem=pe)
            e_bc = pe.n
            HZ["ms_pe"] = e_bc
            qV.wait(pe, e_bc)
            qV.wait(pe, HZ["n_pe"][par])
            for k in range(KC):
                gslot = k % 2
                g = gev[:, gslot * 512:gslot * 512 + w]
                B.vop(lambda e, o=g, a=xv[:, k, 0:w], b=Mb:
                      e.tensor_tensor(o, a, b, ALU.subtract),
                      writes=("gev",))
                ev = B.vop(lambda e, o=nv[:, k, so:so + w], a=g, b=Sb:
                           e.tensor_tensor(o, a, b, ALU.mult),
                           reads=("gev",), writes=("nbuf",), sem=ve)
                HZ["gev_ve"][gslot] = ev
            HZ["xt_ve"] = ev
            B.bank_consumed(bA, ve, ev)
            B.bank_consumed(bB, ve, ev)
        HZ["n_ve"][par] = ev
        return ev

    def ln_stages(src_dram, t0, T, par, ready_key):
        nv = nv2[:, par]
        subs = _subs(T)
        out = [lambda: ln_load(src_dram, t0, subs[0][0], subs[0][1],
                               store_evts.get(ready_key, 0))]
        for i, (so, w) in enumerate(subs):
            def stage(i=i, so=so, w=w):
                ln_sub(par, nv, so, w)
                if i + 1 < len(subs):
                    so2, w2 = subs[i + 1]
                    ln_load(src_dram, t0, so2, w2, store_evts.get(ready_key, 0))
            out.append(stage)
        return out

    def routing(T, par):
        nch = T // 128
        nv = nv2[:, par]
        qT.wait(ve, HZ["n_ve"][par])
        qT.wait(gq, e_gwT)
        bA, bB = HZ["ln_banks"]
        B.bank_acquire(bA)
        PL = PB[bA]
        gw3 = gwT.rearrange("p (k e) -> p k e", k=KC)
        for c in range(nch):
            for k in range(KC):
                B.mm(PL[:, c * E:(c + 1) * E], nv[:, k, c * 128:(c + 1) * 128],
                     gw3[:, k], start=(k == 0), stop=(k == KC - 1),
                     sem=pe if (k == KC - 1 and c == nch - 1) else None)
        e_pl = pe.n
        qS.wait(pe, e_pl)
        qS.wait(ve, HZ["route_ve"])
        e_sig = B.vop(lambda e, o=sall[:, 0:nch * E], i=PL[:, 0:nch * E]:
                      e.activation(o, i, AF.Sigmoid),
                      writes=("sall",), sem=se, eng="scalar")
        B.bank_consumed(bA, se, e_sig)
        qV.wait(se, e_sig)
        s4 = sall.rearrange("p (c g m) -> p c g m", c=CHMAX, g=NG)[:, 0:nch]
        gs3 = r_gs.rearrange("p (c g) -> p c g", c=CHMAX)[:, 0:nch]
        gm3 = r_gm.rearrange("p (c g) -> p c g", c=CHMAX)[:, 0:nch]
        mx2 = r_mx.rearrange("p (c o) -> p c o", c=CHMAX)[:, 0:nch]
        ws2 = r_ws.rearrange("p (c o) -> p c o", c=CHMAX)[:, 0:nch]
        cb3 = r_cb.rearrange("p (c e) -> p c e", c=CHMAX)[:, 0:nch]
        ms3 = r_ms.rearrange("p (c e) -> p c e", c=CHMAX)[:, 0:nch]
        ms4 = r_ms.rearrange("p (c g m) -> p c g m", c=CHMAX, g=NG)[:, 0:nch]
        io8 = iota[:, 0:NG].unsqueeze(1).broadcast_to([128, nch, NG])
        io32 = iota.unsqueeze(1).broadcast_to([128, nch, E])
        qV.wait(sx, e_const)
        B.vop(lambda e, o=gs3, i=s4: e.tensor_reduce(o, i, AX.X, ALU.add),
              reads=("sall",), writes=("r_gs",))
        B.vop(lambda e, o=gm3: e.memset(o, 0.0), writes=("r_gm",))

        def extract(src3, io_b, width, accum):
            B.vop(lambda e, o=mx2, i=src3: e.tensor_reduce(o, i, AX.X, ALU.max),
                  reads=("r_gs", "r_ms"), writes=("r_mx",))
            eq3 = r_t1.rearrange("p (c e) -> p c e", c=CHMAX)[:, 0:nch, 0:width]
            B.vop(lambda e, o=eq3, a=src3, b=mx2.broadcast_to([128, nch, width]):
                  e.tensor_tensor(o, a, b, ALU.is_ge),
                  reads=("r_mx",), writes=("r_t1",))
            si3 = r_t2.rearrange("p (c e) -> p c e", c=CHMAX)[:, 0:nch, 0:width]
            B.vop(lambda e, o=si3, a=eq3, b=io_b:
                  e.scalar_tensor_tensor(o, a, -BIGI, b, ALU.mult, ALU.add),
                  reads=("r_t1",), writes=("r_t2",))
            B.vop(lambda e, o=si3, i=si3: e.tensor_scalar(o, i, BIGI, None, ALU.add),
                  reads=("r_t2",), writes=("r_t2",))
            B.vop(lambda e, o=mx2, i=si3: e.tensor_reduce(o, i, AX.X, ALU.min),
                  reads=("r_t2",), writes=("r_mx",))
            fs3 = r_t1.rearrange("p (c e) -> p c e", c=CHMAX)[:, 0:nch, 0:width]
            B.vop(lambda e, o=fs3, a=io_b, b=mx2.broadcast_to([128, nch, width]):
                  e.tensor_tensor(o, a, b, ALU.is_equal),
                  reads=("r_mx",), writes=("r_t1",))
            accum(fs3)
            B.vop(lambda e, o=src3, a=fs3, b=src3:
                  e.scalar_tensor_tensor(o, a, -BIGI, b, ALU.mult, ALU.add),
                  reads=("r_t1",), writes=("r_gs", "r_ms"))

        for _ in range(TOPG):
            extract(gs3, io8, NG,
                    lambda fs3: B.vop(lambda e, o=gm3, a=gm3, b=fs3:
                                      e.tensor_tensor(o, a, b, ALU.add),
                                      reads=("r_t1", "r_gm"), writes=("r_gm",)))
        B.vop(lambda e, o=ms4, a=s4,
              b=gm3.unsqueeze(3).broadcast_to([128, nch, NG, EPG]):
              e.tensor_tensor(o, a, b, ALU.mult),
              reads=("r_gm", "sall"), writes=("r_ms",))
        B.vop(lambda e, o=cb3: e.memset(o, 0.0), writes=("r_cb",))

        def acc8(fs3):
            B.vop(lambda e, o=r_t2[:, 0:nch * E], a=r_t1[:, 0:nch * E],
                  b=r_ms[:, 0:nch * E]: e.tensor_tensor(o, a, b, ALU.mult),
                  reads=("r_t1", "r_ms"), writes=("r_t2",))
            B.vop(lambda e, o=cb3, a=cb3,
                  b=r_t2.rearrange("p (c e) -> p c e", c=CHMAX)[:, 0:nch]:
                  e.tensor_tensor(o, a, b, ALU.add),
                  reads=("r_t2", "r_cb"), writes=("r_cb",))

        for _ in range(TOPK):
            extract(ms3, io32, E, acc8)
        B.vop(lambda e, o=ws2, i=cb3: e.tensor_reduce(o, i, AX.X, ALU.add),
              reads=("r_cb",), writes=("r_ws",))
        B.vop(lambda e, o=r_ws[:, 0:nch], i=r_ws[:, 0:nch]:
              e.tensor_scalar(o, i, 1e-12, None, ALU.max),
              reads=("r_ws",), writes=("r_ws",))
        B.vop(lambda e, o=r_ws[:, 0:nch], i=r_ws[:, 0:nch]: e.reciprocal(o, i),
              reads=("r_ws",), writes=("r_ws",))
        e_cb = B.vop(lambda e, o=cb3, a=cb3, b=ws2.broadcast_to([128, nch, E]):
                     e.tensor_tensor(o, a, b, ALU.mult),
                     reads=("r_ws", "r_cb"), writes=("r_cb",), sem=ve)
        HZ["e_cb"] = e_cb
        return e_cb

    def routing_finish(T, par):
        nch = T // 128
        qT.wait(ve, HZ["e_cb"])
        qT.wait(sx, e_const)
        bA, bB = HZ["ln_banks"]
        B.bank_acquire(bA)
        B.bank_acquire(bB)
        for c in range(nch):
            PT = PB[[bA, bB][c // 4]][0:E, (c % 4) * 128:(c % 4 + 1) * 128]
            last = (c == nch - 1)
            B.q["tensor"].op(lambda e, o=PT,
                             i=r_cb.rearrange("p (c e) -> p c e", c=CHMAX)[:, c],
                             d=ident: e.transpose(o, i, d).then_inc(pe.h, 1))
            pe.bump()
        qV.wait(pe, pe.n)
        qV.wait(sx, HZ["cex_sx"][par])
        e_tr = 0
        for b in range((nch + 3) // 4):
            wb = min(4, nch - b * 4) * 128
            e_tr = B.vop(lambda e, o=cex2[:, par, b * 512:b * 512 + wb],
                         i=PB[[bA, bB][b]][0:E, 0:wb]:
                         e.tensor_copy(o, i), writes=("cex",), sem=ve)
            B.bank_consumed([bA, bB][b], ve, e_tr)
        HZ["route_ve"] = e_tr
        HZ["e_cex"][par] = e_tr
        return e_tr

    store_evts = {}

    class OnceStage:
        def __init__(self, f):
            self.f, self.done = f, False

        def __call__(self):
            if not self.done:
                self.done = True
                self.f()

    def mk_stages(kind, src, t0, T, par, ready_key):
        st_ = ln_stages(src, t0, T, par, ready_key)
        if kind == "lnr":
            st_.append(lambda: routing(T, par))
            st_.append(lambda: routing_finish(T, par))
        return [OnceStage(f) for f in st_]

    def load_expert(ex, e_wld):
        bgu = ex % 2
        bd = ex % 3
        B.dma("gpsimd", wg_v[:, bgu], wg_h[ex].rearrange("(k p) n -> p k n", p=128),
              gq, waits=[(pe, HZ["wgbuf_pe"].get(bgu, 0))])
        B.dma("gpsimd", wu_v[:, bgu], wu_h[ex].rearrange("(k p) n -> p k n", p=128), gq)
        e_wld[ex] = B.dma("gpsimd", wd_v[:, bd],
                          wd_h[ex].rearrange("(k p) n -> p k n", p=128),
                          gq, waits=[(pe, HZ["wdbuf_pe"].get(bd, 0))])

    def down_m(ex, s_i, slot, subs, av_, m):
        so, w = subs[s_i]
        bd = ex % 3
        bk = YBANKS[m % 3]
        B.bank_acquire(bk)
        yB = PB[bk]
        for ic in range(IC):
            B.mm(yB[:, 0:w], wd_v[:, bd, ic, m * 128:(m + 1) * 128],
                 hs_v[:, slot, ic, 0:w],
                 start=(ic == 0), stop=(ic == IC - 1),
                 sem=pe if ic == IC - 1 else None)
        last_pe = pe.n
        qV.wait(pe, last_pe)
        ev = B.vop(lambda e, o=av_[:, m, so:so + w], a=av_[:, m, so:so + w],
                   b=yB[:, 0:w]: e.tensor_tensor(o, a, b, ALU.add),
                   reads=("acc",), writes=("acc",), sem=ve)
        B.bank_consumed(bk, ve, ev)
        return last_pe

    def down_expert(ex, s_i, slot, subs, av_, e_hwr, e_down, e_down_end,
                    ms=range(KC)):
        qT.wait(ve, e_hwr[(ex, s_i)])
        last_pe = None
        for m in ms:
            last_pe = down_m(ex, s_i, slot, subs, av_, m)
        e_down[(ex, s_i)] = last_pe
        HZ["hslot_pe"][slot] = last_pe
        if s_i == len(subs) - 1:
            e_down_end[ex] = last_pe
            HZ["wdbuf_pe"][ex % 3] = last_pe

    def ffn_tile(t0, T, par, src_dram, dst_dram, next_stages=(), own_stages=()):
        HZ["ln_banks"] = (0, 1)
        for s in own_stages:
            s()
        subs = _subs(T)
        nv = nv2[:, par]
        e_cex = HZ["e_cex"][par]
        e_f0 = B.vop(lambda e, o=acc[:, :]: e.memset(o, 0.0),
                     writes=("acc",), sem=ve)
        e_wld, e_hwr, e_down, e_cstv = {}, {}, {}, {}
        e_down_end = {}

        def load_cst(ex):
            if ex >= E:
                return
            le = ex % 2
            e_cstv[ex] = B.dma(
                "sync", cst[0:1, le * TMAX:le * TMAX + T],
                cex2[ex:ex + 1, par, 0:T].squeeze(1) if False else cex[ex:ex + 1, par * TMAX:par * TMAX + T],
                sx, waits=[(ve, e_cex), (pe, HZ["cst_pe"][le])])
            HZ["cex_sx"][par] = e_cstv[ex]

        load_expert(0, e_wld)
        load_expert(1, e_wld)
        load_cst(0)
        load_cst(1)
        items = [(ex, s_i) for ex in range(NEXP) for s_i in range(len(subs))]
        for idx, (ex, s_i) in enumerate(items):
            so, w = subs[s_i]
            le = ex % 2
            bgu = ex % 2
            use_c = ex < E
            if use_c:
                qT.wait(sx, e_cstv[ex])
                B.bank_acquire(6)
                CB = PB[6][:, 0:w]
                B.mm(CB, onesr, cst[0:1, le * TMAX + so:le * TMAX + so + w],
                     start=True, stop=True, sem=pe)
                e_cbank = pe.n
                HZ["cst_pe"][le] = e_cbank
                qS.wait(pe, e_cbank)
                qS.wait(ve, HZ["csb_ve"][le])
                e_cready = B.vop(
                    lambda e, o=c_sb[:, le * 512:le * 512 + w], i=CB:
                    e.copy(o, i), writes=("c_sb",), sem=se, eng="scalar")
                B.bank_consumed(6, se, e_cready)
            qT.wait(gq, e_wld[ex])
            qT.wait(ve, HZ["n_ve"][par])
            if idx > 0:
                pex, ps = items[idx - 1]
            for ic in range(IC):
                gbk, ubk = ic % 2, 2 + ic % 2
                B.bank_acquire(gbk)
                B.bank_acquire(ubk)
                gB, uB = PB[gbk], PB[ubk]
                for k in range(KC):
                    B.mm(gB[:, 0:w], wg_v[:, bgu, k, ic * 128:(ic + 1) * 128],
                         nv[:, k, so:so + w], start=(k == 0),
                         stop=(k == KC - 1), sem=pe if k == KC - 1 else None)
                e_g = pe.n
                for k in range(KC):
                    B.mm(uB[:, 0:w], wu_v[:, bgu, k, ic * 128:(ic + 1) * 128],
                         nv[:, k, so:so + w], start=(k == 0),
                         stop=(k == KC - 1), sem=pe if k == KC - 1 else None)
                e_u = pe.n
                qS.wait(pe, e_g)
                qS.wait(ve, HZ["gs_ve"][ic])
                e_gs = B.vop(lambda e, o=gs_scr[:, ic * 512:ic * 512 + w],
                             i=gB[:, 0:w]: e.activation(o, i, AF.Silu),
                             writes=("gs_scr",), sem=se, eng="scalar")
                B.bank_consumed(gbk, se, e_gs)
                qV.wait(pe, e_u)
                if use_c:
                    qV.wait(se, e_cready)
                    e_uc = B.vop(lambda e, o=uc_scr[:, ic * 512:ic * 512 + w],
                                 a=uB[:, 0:w], b=c_sb[:, le * 512:le * 512 + w]:
                                 e.tensor_tensor(o, a, b, ALU.mult),
                                 writes=("uc_scr",), sem=ve)
                    HZ["csb_ve"][le] = e_uc
                else:
                    e_uc = B.vop(lambda e, o=uc_scr[:, ic * 512:ic * 512 + w],
                                 i=uB[:, 0:w]: e.tensor_copy(o, i),
                                 writes=("uc_scr",), sem=ve)
                B.bank_consumed(ubk, ve, e_uc)
                if idx > 0:
                    qT.wait(ve, e_hwr[(pex, ps)])
                    down_m(pex, ps, (idx - 1) % 2, subs, av, 2 * ic)
                    down_m(pex, ps, (idx - 1) % 2, subs, av, 2 * ic + 1)
            qV.wait(se, se.n)
            qV.wait(pe, HZ["hslot_pe"].get(idx % 2, 0))
            qV.wait(ve, e_f0)
            for ic in range(IC):
                ev = B.vop(lambda e, o=hs_v[:, idx % 2, ic, 0:w],
                           a=gs_scr[:, ic * 512:ic * 512 + w],
                           b=uc_scr[:, ic * 512:ic * 512 + w]:
                           e.tensor_tensor(o, a, b, ALU.mult),
                           reads=("gs_scr", "uc_scr"), writes=("hslots",),
                           sem=ve)
                HZ["gs_ve"][ic] = ev
            e_hwr[(ex, s_i)] = ev
            if idx > 0:
                e_down[(pex, ps)] = pe.n
                HZ["hslot_pe"][(idx - 1) % 2] = pe.n
                if ps == len(subs) - 1:
                    e_down_end[pex] = pe.n
                    HZ["wdbuf_pe"][pex % 3] = pe.n
            if s_i == len(subs) - 1:
                HZ["wgbuf_pe"][ex % 2] = pe.n
                if not HZ["bf16w"] and ex not in HZ["e_cast"]:
                    B.dma("gpsimd", wgs_h[ex], wg_h[ex], gq)
                    B.dma("gpsimd", wus_h[ex], wu_h[ex], gq)
                    HZ["e_cast"][ex] = B.dma("gpsimd", wds_h[ex], wd_h[ex], gq)
                if ex + 2 < NEXP:
                    load_expert(ex + 2, e_wld)
                if ex + 2 < E:
                    load_cst(ex + 2)
                if ex >= 12 and (ex - 12) % 3 == 0:
                    k = (ex - 12) // 3
                    if k < len(next_stages):
                        next_stages[k]()
        pex, ps = items[-1]
        down_expert(pex, ps, (len(items) - 1) % 2, subs, av, e_hwr, e_down,
                    e_down_end)
        HZ["n_pe"][par] = pe.n
        e_acc_done = ve.n
        last_st = 0
        for s_i, (so, w) in enumerate(subs):
            e_ld = B.dma("sync", xv[:, :, 0:w],
                         src_dram[:, t0 + so:t0 + so + w].rearrange("(k p) t -> p k t", p=128),
                         sx, waits=[(ve, HZ["xt_ve"]), (sx, HZ["xt_sx"])])
            qV.wait(sx, e_ld)
            qV.wait(ve, e_acc_done)
            for k in range(KC):
                ev = B.vop(lambda e, o=xv[:, k, 0:w], a=av[:, k, so:so + w],
                           b=xv[:, k, 0:w]:
                           e.scalar_tensor_tensor(o, a, 0.5, b, ALU.mult, ALU.add),
                           reads=("acc",), writes=("xt",), sem=ve)
            HZ["xt_ve"] = ev
            last_st = B.dma(
                "sync", dst_dram[:, t0 + so:t0 + so + w].rearrange("(k p) t -> p k t", p=128),
                xv[:, :, 0:w], sx, waits=[(ve, ev)])
            HZ["xt_sx"] = last_st
        return last_st

    def griffin_tile(t0, T, par, t_i, e_wld_g, e_pm, e_wo, next_stages=(),
                     own_stages=()):
        HZ["ln_banks"] = (4, 5)
        for s in own_stages:
            s()
        subs = _subs(T)
        nv = nv2[:, par]

        def load_gw(m):
            buf = m % 2
            B.dma("gpsimd", wr_v[:, buf],
                  wr_h[:, m * 128:(m + 1) * 128].rearrange("(k p) j -> p k j", p=128),
                  gq, waits=[(pe, e_pm.get((t_i, m - 2), 0) if m >= 2
                              else e_pm.get((t_i - 1, KC - 2 + m), 0))])
            B.dma("gpsimd", wi_v[:, buf],
                  wi_h[:, m * 128:(m + 1) * 128].rearrange("(k p) j -> p k j", p=128), gq)
            e_wld_g[(t_i, m)] = B.dma(
                "gpsimd", wn_v[:, buf],
                win_h[:, m * 128:(m + 1) * 128].rearrange("(k p) j -> p k j", p=128), gq)

        load_gw(0)
        load_gw(1)
        for m in range(KC):
            buf = m % 2
            qT.wait(gq, e_wld_g[(t_i, m)])
            qT.wait(ve, HZ["n_ve"][par])
            for s_i, (so, w) in enumerate(subs):
                j = (m * len(subs) + s_i) % 2
                rbk, ibk, pbk = [0, 3][j], [1, 6][j], [2, 7][j]
                B.bank_acquire(rbk)
                RB = PB[rbk]
                for k in range(KC):
                    B.mm(RB[:, 0:w], wr_v[:, buf, k], nv[:, k, so:so + w],
                         start=(k == 0), stop=(k == KC - 1),
                         sem=pe if k == KC - 1 else None)
                e_r = pe.n
                B.bank_acquire(ibk)
                IB = PB[ibk]
                for k in range(KC):
                    B.mm(IB[:, 0:w], wi_v[:, buf, k], nv[:, k, so:so + w],
                         start=(k == 0), stop=(k == KC - 1),
                         sem=pe if k == KC - 1 else None)
                e_i = pe.n
                B.bank_acquire(pbk)
                PP = PB[pbk]
                for k in range(KC):
                    B.mm(PP[:, 0:w], wn_v[:, buf, k], nv[:, k, so:so + w],
                         start=(k == 0), stop=(k == KC - 1),
                         sem=pe if k == KC - 1 else None)
                e_p = pe.n
                e_pm[(t_i, m)] = e_p
                gslot = 0
                qS.wait(pe, e_r)
                qS.wait(se, e_avec)
                qS.wait(ve, HZ["gev_ve"][gslot])
                e_sg = B.vop(lambda e, o=gev[:, 0:512][:, 0:w], i=RB[:, 0:w],
                             b=brT[:, m:m + 1]: e.activation(o, i, AF.Sigmoid, bias=b),
                             writes=("gev",), sem=se, eng="scalar")
                B.bank_consumed(rbk, se, e_sg)
                qV.wait(se, e_sg)
                e_a = B.vop(lambda e, o=av[:, m, so:so + w], i=gev[:, 0:512][:, 0:w],
                            s=avec[:, m:m + 1]: e.tensor_scalar(o, i, s, None, ALU.mult),
                            writes=("acc",), sem=ve)
                HZ["gev_ve"][gslot] = e_a
                qS.wait(pe, e_i)
                qS.wait(ve, HZ["igs_ve"])
                e_ig = B.vop(lambda e, o=igs[:, 0:w], i=IB[:, 0:w], b=biT[:, m:m + 1]:
                             e.activation(o, i, AF.Sigmoid, bias=b),
                             writes=("igs",), sem=se, eng="scalar")
                B.bank_consumed(ibk, se, e_ig)
                qS.wait(pe, e_p)
                qS.wait(ve, HZ["ups_ve"])
                e_p1 = B.vop(lambda e, o=ups[:, 0:w], i=PP[:, 0:w], b=binT[:, m:m + 1]:
                             e.activation(o, i, AF.Identity, bias=b),
                             writes=("ups",), sem=se, eng="scalar")
                B.bank_consumed(pbk, se, e_p1)
                qV.wait(se, e_p1)
                e_um = B.vop(lambda e, o=ups[:, 0:w], a=igs[:, 0:w], b=ups[:, 0:w]:
                             e.tensor_tensor(o, a, b, ALU.mult),
                             writes=("ups",), sem=ve)
                HZ["igs_ve"] = e_um
                ev = B.vop(lambda e, o=bv[:, m, so:so + w], a=av[:, m, so:so + w],
                           b=ups[:, 0:w]:
                           e.scalar_tensor_tensor(o, a, -1.0, b, ALU.add, ALU.mult),
                           reads=("ups", "acc"), writes=("bbuf",), sem=ve)
                HZ["ups_ve"] = ev
            if t_i == 0:
                qV.wait(sx, e_const)
                B.vop(lambda e, o=av[:, m, 0:WARM], i=av[:, m, 0:WARM],
                      s=maskc[:, 0:1]: e.tensor_scalar(o, i, s, None, ALU.mult),
                      reads=("acc",), writes=("acc",))
                B.vop(lambda e, o=bv[:, m, 0:WARM], i=bv[:, m, 0:WARM],
                      s=maskc[:, 0:1]: e.tensor_scalar(o, i, s, None, ALU.mult),
                      reads=("bbuf",), writes=("bbuf",))
            qV.wait(ve, e_hc0)
            qV.wait(pe, HZ["hb_pe"])
            B.vop(lambda e, o=hscr[:, 0:T], a=av[:, m, 0:T], b=bv[:, m, 0:T],
                  i=hcar[:, m:m + 1]:
                  e.tensor_tensor_scan(o, a, b, i, ALU.mult, ALU.add),
                  reads=("acc", "bbuf", "hcar"), writes=("hscr",))
            B.vop(lambda e, o=hcar[:, m:m + 1], i=hscr[:, T - 1:T]: e.tensor_copy(o, i),
                  reads=("hscr",), writes=("hcar",))
            e_h = B.vop(lambda e, o=hv[:, m, 0:T], i=hscr[:, 0:T]: e.tensor_copy(o, i),
                        reads=("hscr",), writes=("hb16",), sem=ve)
            e_pm[(t_i, m, "h")] = e_h
            if m + 2 < KC:
                load_gw(m + 2)
            if m in (1, 3, 5):
                k = (m - 1) // 2
                if k < len(next_stages):
                    next_stages[k]()
        HZ["n_pe"][par] = pe.n
        qT.wait(gq, e_wo)
        for k in range(KC):
            qT.wait(ve, e_pm[(t_i, k, "h")])
        last_pe_h = 0
        last_st = 0
        for s_i, (so, w) in enumerate(subs):
            e_x1 = B.dma("sync", xv[:, :, 0:w],
                         x1_h[:, t0 + so:t0 + so + w].rearrange("(k p) t -> p k t", p=128),
                         sx, waits=[(ve, HZ["xt_ve"]), (sx, HZ["xt_sx"])])
            ev = 0
            for m2 in range(KC):
                bk = 2 + (m2 % 2)
                B.bank_acquire(bk)
                GB = PB[bk]
                for k in range(KC):
                    B.mm(GB[:, 0:w], wo_v[:, m2, k], hv[:, k, so:so + w],
                         start=(k == 0), stop=(k == KC - 1),
                         sem=pe if k == KC - 1 else None)
                e_g = pe.n
                last_pe_h = e_g
                gslot = m2 % 2
                qS.wait(pe, e_g)
                qS.wait(ve, HZ["gev_ve"][gslot])
                e_gv = B.vop(lambda e, o=gev[:, gslot * 512:gslot * 512 + w],
                             i=GB[:, 0:w], b=boT[:, m2:m2 + 1]:
                             e.activation(o, i, AF.Identity, bias=b, scale=-1.0),
                             writes=("gev",), sem=se, eng="scalar")
                B.bank_consumed(bk, se, e_gv)
                qV.wait(se, e_gv)
                qV.wait(sx, e_x1)
                ev = B.vop(lambda e, o=xv[:, m2, 0:w], a=xv[:, m2, 0:w],
                           b=gev[:, gslot * 512:gslot * 512 + w]:
                           e.tensor_tensor(o, a, b, ALU.add),
                           writes=("xt",), sem=ve)
                HZ["gev_ve"][gslot] = ev
            HZ["xt_ve"] = ev
            lo = max(so, WARM) if t0 == 0 else so
            g0 = t0 + lo - WARM
            ww = so + w - lo
            if ww > 0:
                last_st = B.dma(
                    "sync", x2_h[:, g0:g0 + ww].rearrange("(k p) t -> p k t", p=128),
                    xv[:, :, lo - so:lo - so + ww], sx, waits=[(ve, ev)])
                HZ["xt_sx"] = last_st
            for k in range(3 + s_i, len(next_stages)):
                if k - 3 == s_i:
                    next_stages[k]()
        HZ["hb_pe"] = last_pe_h
        return last_st

    starts_A = np.cumsum([0] + TILES_A[:-1]).tolist()
    starts_B = np.cumsum([0] + TILES_B[:-1]).tolist()
    starts_C = np.cumsum([0] + TILES_C[:-1]).tolist()
    n_jobs = len(TILES_A) + len(TILES_B) + len(TILES_C)
    pars = [j % 2 for j in range(n_jobs)]
    stages = []
    ji = 0
    for t_i, (t0, T) in enumerate(zip(starts_A, TILES_A)):
        stages.append(mk_stages("lnr", x_h, t0, T, pars[ji], None))
        ji += 1
    for t_i, (t0, T) in enumerate(zip(starts_B, TILES_B)):
        stages.append(mk_stages("ln", x1_h, t0, T, pars[ji], ("A", t_i)))
        ji += 1
    for t_i, (t0, T) in enumerate(zip(starts_C, TILES_C)):
        stages.append(mk_stages("lnr", x2_h, t0, T, pars[ji],
                                ("B", min(t_i + 1, len(TILES_B) - 1))))
        ji += 1

    for s in stages[0]:
        s()
    for t_i, (t0, T) in enumerate(zip(starts_A, TILES_A)):
        par = pars[t_i]
        nxt = stages[t_i + 1] if t_i + 1 < len(stages) else ()
        store_evts[("A", t_i)] = ffn_tile(t0, T, par, x_h, x1_h, nxt,
                                          stages[t_i])
        if t_i == 0:
            HZ["bf16w"] = True
    A_end_pe = pe.n
    qG.wait(pe, pe.n)
    qG.wait(ve, ve.n)
    qG.wait(se, se.n)
    qV.wait(pe, A_end_pe)
    qS.wait(pe, A_end_pe)
    e_wo = 0
    for m2 in range(KC):
        e_wo = B.dma("gpsimd", wo_v[:, m2],
                     wo_h[:, m2 * 128:(m2 + 1) * 128].rearrange("(k p) j -> p k j", p=128),
                     gq)
    e_wld_g, e_pm_g = {}, {}
    for t_i, (t0, T) in enumerate(zip(starts_B, TILES_B)):
        ji = len(TILES_A) + t_i
        par = pars[ji]
        nxt = stages[ji + 1] if ji + 1 < len(stages) else ()
        store_evts[("B", t_i)] = griffin_tile(t0, T, par, t_i, e_wld_g, e_pm_g,
                                              e_wo, nxt, stages[ji])
    B_end_pe = pe.n
    qG.wait(pe, pe.n)
    qG.wait(ve, ve.n)
    qG.wait(se, se.n)
    qV.wait(pe, B_end_pe)
    qS.wait(pe, B_end_pe)
    HZ["wgbuf_pe"] = {}
    HZ["wdbuf_pe"] = {}
    HZ["hslot_pe"] = {}
    for t_i, (t0, T) in enumerate(zip(starts_C, TILES_C)):
        ji = len(TILES_A) + len(TILES_B) + t_i
        par = pars[ji]
        nxt = stages[ji + 1] if ji + 1 < len(stages) else ()
        ffn_tile(t0, T, par, x2_h, out_h, nxt, stages[ji])
    B.q["sync"].op(lambda e, v=sx.n: e.wait_ge(sx.h, v))

    with st:
        with nc.Block() as block:
            @block.sync
            def _(eng):
                for f in B.q["sync"].ops:
                    f(eng)

            @block.gpsimd
            def _(eng):
                for f in B.q["gpsimd"].ops:
                    f(eng)

            @block.tensor
            def _(eng):
                for f in B.q["tensor"].ops:
                    f(eng)

            @block.vector
            def _(eng):
                for f in B.q["vector"].ops:
                    f(eng)

            @block.scalar
            def _(eng):
                for f in B.q["scalar"].ops:
                    f(eng)
    return nc


def kernel(L0_attn_x, L0_attn_W_in, L0_attn_b_in, L0_attn_W_r, L0_attn_b_r,
           L0_attn_W_i, L0_attn_b_i, L0_attn_W_out, L0_attn_b_out,
           L0_attn_log_a, L0_ffn_gate_w, L0_ffn_expert_gate,
           L0_ffn_expert_up, L0_ffn_expert_down, L0_ffn_shared_gate_w,
           L0_ffn_shared_up_w, L0_ffn_shared_down_w):
    f32 = np.float32
    x = np.asarray(L0_attn_x, f32)
    xT = np.ascontiguousarray(np.swapaxes(x, 1, 2))

    def lay(v):
        return np.ascontiguousarray(np.asarray(v, f32))

    wg = np.ascontiguousarray(np.concatenate(
        [np.swapaxes(lay(L0_ffn_expert_gate), 1, 2),
         lay(L0_ffn_shared_gate_w).T[None]], 0))
    wu = np.ascontiguousarray(np.concatenate(
        [np.swapaxes(lay(L0_ffn_expert_up), 1, 2),
         lay(L0_ffn_shared_up_w).T[None]], 0))
    wd = np.ascontiguousarray(np.concatenate(
        [np.swapaxes(lay(L0_ffn_expert_down), 1, 2),
         lay(L0_ffn_shared_down_w).T[None]], 0))
    gw = np.ascontiguousarray(lay(L0_ffn_gate_w).T)

    def vec_pk(v):
        return np.ascontiguousarray(lay(v).reshape(KC, 128).T)

    base = {
        "wg": wg, "wu": wu, "wd": wd, "gw": gw,
        "win": lay(L0_attn_W_in), "wr": lay(L0_attn_W_r),
        "wi": lay(L0_attn_W_i), "wo": lay(L0_attn_W_out),
        "bin": vec_pk(L0_attn_b_in), "br": vec_pk(L0_attn_b_r),
        "bi": vec_pk(L0_attn_b_i), "bo": vec_pk(L0_attn_b_out),
        "la": vec_pk(L0_attn_log_a),
        "iota": np.tile(np.arange(E, dtype=f32), (128, 1)),
        "ident": np.eye(128, dtype=f32),
    }
    nc = build_program()
    in_maps = []
    for c in range(8):
        b, half = c // 2, c % 2
        if half == 0:
            xs = np.concatenate([xT[b][:, 0:WARM], xT[b][:, 0:OWN]], axis=1)
            mk = np.zeros((128, 1), f32)
        else:
            xs = xT[b][:, OWN - WARM:2 * OWN]
            mk = np.ones((128, 1), f32)
        m = dict(base)
        m["x"] = np.ascontiguousarray(xs)
        m["mask"] = mk
        in_maps.append(m)

    res = run_bass_kernel_spmd(nc, in_maps, core_ids=list(range(8)))
    out = np.empty((4, 1024, 8192), f32)
    for c in range(8):
        b, half = c // 2, c % 2
        out[b][:, half * OWN:(half + 1) * OWN] = res.results[c]["out"]
    return np.ascontiguousarray(np.swapaxes(out, 1, 2))


# version 14 (final)
# speedup vs baseline: 1.1704x; 1.1704x over previous
import numpy as np
import concourse.bass as bass
import concourse.mybir as mybir
from concourse.bass_utils import run_bass_kernel_spmd
from contextlib import ExitStack

F32 = mybir.dt.float32
BF16 = mybir.dt.bfloat16
AF = mybir.ActivationFunctionType
ALU = mybir.AluOpType
AX = mybir.AxisListType

D = 1024
I = 512
E = 32
NEXP = E + 1
NG, EPG, TOPG, TOPK = 8, 4, 4, 8
WARM = 128
OWN = 4096
SLICE = WARM + OWN
KC = 8
IC = 4
EPS = 1e-5
BIGI = 1.0e4

TILES_A = [896, 896, 896, 768, 768]
TILES_B = [896, 896, 896, 768, 768]
TILES_C = [1024, 1024, 1024, 1024]
TMAX = 1024
CHMAX = TMAX // 128
YBANKS = [4, 5, 7]


def _subs(T):
    out, o = [], 0
    while o < T:
        w = min(512, T - o)
        out.append((o, w))
        o += w
    return out


class Sem:
    def __init__(self, h):
        self.h = h
        self.n = 0

    def bump(self, k=1):
        self.n += k
        return self.n


class EQ:
    def __init__(self):
        self.ops = []
        self.hw = {}

    def op(self, fn):
        self.ops.append(fn)

    def wait(self, sem, val):
        if val is None or val <= 0:
            return
        assert val <= sem.n, "wait on future event (plan bug)"
        if self.hw.get(id(sem), -1) >= val:
            return
        self.hw[id(sem)] = val
        self.op(lambda e, s=sem.h, v=val: e.wait_ge(s, v))

    def waits(self, evts):
        for sem, val in evts:
            self.wait(sem, val)


class Builder:
    def __init__(self, nc):
        self.nc = nc
        self.q = {k: EQ() for k in ("sync", "gpsimd", "tensor", "vector", "scalar")}
        self.dirty = {"vector": set(), "scalar": set(), "gpsimd": set()}
        self.bank = {i: [] for i in range(8)}

    def dma(self, queue, out, in_, sem, waits=()):
        self.q[queue].waits(waits)
        self.q[queue].op(lambda e, o=out, i=in_, s=sem.h:
                         e.dma_start(out=o, in_=i).then_inc(s, 16))
        return sem.bump(16)

    def mm(self, out, lhsT, rhs, start, stop, sem=None):
        if sem is None:
            self.q["tensor"].op(lambda e, o=out, l=lhsT, r=rhs, a=start, b=stop:
                                e.matmul(o, l, r, start=a, stop=b))
            return None
        self.q["tensor"].op(lambda e, o=out, l=lhsT, r=rhs, a=start, b=stop, s=sem.h:
                            e.matmul(o, l, r, start=a, stop=b).then_inc(s, 1))
        return sem.bump()

    def bank_acquire(self, i):
        self.q["tensor"].waits(self.bank[i])
        self.bank[i] = []

    def bank_consumed(self, i, sem, evt):
        self.bank[i].append((sem, evt))

    def vop(self, fn, reads=(), writes=(), sem=None, eng="vector"):
        d = self.dirty[eng]
        if any(r in d for r in reads):
            self.q[eng].op(lambda e: e.drain())
            d.clear()
        for w in writes:
            d.add(w)
        if sem is None:
            self.q[eng].op(fn)
            return None
        self.q[eng].op(lambda e, f=fn, s=sem.h: f(e).then_inc(s, 1))
        return sem.bump()


def build_program():
    nc = bass.Bass("TRN2", target_bir_lowering=False)
    st = ExitStack()

    x_h = nc.dram_tensor("x", [D, SLICE], F32, kind="ExternalInput").ap()
    wg_h = nc.dram_tensor("wg", [E + 1, D, I], F32, kind="ExternalInput").ap()
    wu_h = nc.dram_tensor("wu", [E + 1, D, I], F32, kind="ExternalInput").ap()
    wd_h = nc.dram_tensor("wd", [E + 1, I, D], F32, kind="ExternalInput").ap()
    gw_h = nc.dram_tensor("gw", [D, E], F32, kind="ExternalInput").ap()
    win_h = nc.dram_tensor("win", [D, D], F32, kind="ExternalInput").ap()
    wr_h = nc.dram_tensor("wr", [D, D], F32, kind="ExternalInput").ap()
    wi_h = nc.dram_tensor("wi", [D, D], F32, kind="ExternalInput").ap()
    wo_h = nc.dram_tensor("wo", [D, D], F32, kind="ExternalInput").ap()
    bin_h = nc.dram_tensor("bin", [128, KC], F32, kind="ExternalInput").ap()
    br_h = nc.dram_tensor("br", [128, KC], F32, kind="ExternalInput").ap()
    bi_h = nc.dram_tensor("bi", [128, KC], F32, kind="ExternalInput").ap()
    bo_h = nc.dram_tensor("bo", [128, KC], F32, kind="ExternalInput").ap()
    la_h = nc.dram_tensor("la", [128, KC], F32, kind="ExternalInput").ap()
    io_h = nc.dram_tensor("iota", [128, E], F32, kind="ExternalInput").ap()
    id_h = nc.dram_tensor("ident", [128, 128], F32, kind="ExternalInput").ap()
    mk_h = nc.dram_tensor("mask", [128, 1], F32, kind="ExternalInput").ap()
    out_h = nc.dram_tensor("out", [D, OWN], F32, kind="ExternalOutput").ap()
    x1_h = nc.dram_tensor("x1scr", [D, SLICE], F32, kind="Internal").ap()
    x2_h = nc.dram_tensor("x2scr", [D, OWN], F32, kind="Internal").ap()
    wgs_h = nc.dram_tensor("wgscr", [E + 1, D, I], BF16, kind="Internal").ap()
    wus_h = nc.dram_tensor("wuscr", [E + 1, D, I], BF16, kind="Internal").ap()
    wds_h = nc.dram_tensor("wdscr", [E + 1, I, D], BF16, kind="Internal").ap()

    POOL = 206 * 1024
    pool_h = st.enter_context(nc.sbuf_tensor("pool", [128, POOL], mybir.dt.uint8))
    pool_addr = nc.lookup_mloc(pool_h).addr
    cur = [0]

    def carve(name, shape, dtype, at=None):
        per = int(np.prod(shape[1:])) * mybir.dt.size(dtype)
        if at is None:
            at = cur[0]
            cur[0] = (at + per + 31) & ~31
        assert at + per <= POOL, f"SBUF pool overflow at {name}: {at + per}"
        hnd = nc.virtual_sbuf_tensor(name, shape, dtype)
        nc._tensor(hnd.name, hnd.shape, hnd.dtype, type="SB")
        mloc = nc.lookup_mloc(hnd)
        mloc.allocated = True
        mloc.addr = pool_addr + at
        return hnd.ap()

    ident = carve("ident", [128, 128], F32)
    iota = carve("iota", [128, E], F32)
    onesc = carve("onesc", [128, 1], F32)
    onesr = carve("onesr", [1, 128], F32)
    onesc_b = carve("onesc_b", [128, 1], BF16)
    onesr_b = carve("onesr_b", [1, 128], BF16)
    mrow_b = carve("mrow_b", [1, 512], BF16)
    binT = carve("binT", [128, KC], F32)
    brT = carve("brT", [128, KC], F32)
    biT = carve("biT", [128, KC], F32)
    boT = carve("boT", [128, KC], F32)
    laT = carve("laT", [128, KC], F32)
    avec = carve("avec", [128, KC], F32)
    maskc = carve("maskc", [128, 1], F32)
    gwT = carve("gwT", [128, KC * E], BF16)
    hcar = carve("hcar", [128, KC], F32)
    nbuf = carve("nbuf", [128, 2 * KC * TMAX], BF16)
    acc = carve("acc", [128, KC * TMAX], F32)
    xt = carve("xt", [128, KC * 512], F32)
    sqs = carve("sqs", [128, 2 * 512], BF16)
    xb_all = carve("xb_all", [128, KC * 512], BF16)
    mrow = carve("mrow", [1, 512], F32)
    vrow = carve("vrow", [1, 512], F32)
    srow = carve("srow", [1, 512], F32)
    gev = carve("gev", [128, 2 * 512], F32)
    ovl0 = cur[0]

    hslots = carve("hslots", [128, 2 * IC * 512], BF16)
    wgbuf = carve("wgbuf", [128, 2 * KC * I], BF16)
    wubuf = carve("wubuf", [128, 2 * KC * I], BF16)
    wdbuf = carve("wdbuf", [128, 3 * IC * D], BF16)
    gs_scr = carve("gs_scr", [128, IC * 512], F32)
    uc_scr = carve("uc_scr", [128, IC * 512], BF16)
    c_sb = carve("c_sb", [128, 2 * 512], F32)
    cst = carve("cst", [1, 2 * TMAX], F32)
    ffn_low = cur[0]

    cur[0] = ovl0
    bbuf = carve("bbuf", [128, KC * TMAX], F32)
    hb16 = carve("hb16", [128, KC * TMAX], BF16)
    hscr = carve("hscr", [128, TMAX], F32)
    igs = carve("igs", [128, 512], F32)
    ups = carve("ups", [128, 512], F32)
    wrbuf = carve("wrbuf", [128, 2 * KC * 128], BF16)
    wibuf = carve("wibuf", [128, 2 * KC * 128], BF16)
    wnbuf = carve("wnbuf", [128, 2 * KC * 128], BF16)
    wobuf = carve("wobuf", [128, KC * KC * 128], BF16)
    grf_end = cur[0]

    cur[0] = max(ffn_low, grf_end)
    cex = carve("cex", [32, 2 * TMAX], F32)
    sall = carve("sall", [128, CHMAX * E], F32)
    r_ms = carve("r_ms", [128, CHMAX * E], F32)
    r_t1 = carve("r_t1", [128, CHMAX * E], F32)
    r_t2 = carve("r_t2", [128, CHMAX * E], F32)
    r_gs = carve("r_gs", [128, CHMAX * NG], F32)
    r_gm = carve("r_gm", [128, CHMAX * NG], F32)
    r_mx = carve("r_mx", [128, CHMAX], F32)
    r_cb = carve("r_cb", [128, CHMAX * E], F32)
    r_ws = carve("r_ws", [128, CHMAX], F32)
    assert cur[0] <= POOL, f"pool overflow: {cur[0]}"

    PB = [st.enter_context(nc.psum_tensor(f"pb{i}", [128, 512], F32)).ap()
          for i in range(8)]

    sx = Sem(st.enter_context(nc.semaphore("sx")))
    gq = Sem(st.enter_context(nc.semaphore("gq")))
    pe = Sem(st.enter_context(nc.semaphore("pe")))
    ve = Sem(st.enter_context(nc.semaphore("ve")))
    se = Sem(st.enter_context(nc.semaphore("se")))
    po = Sem(st.enter_context(nc.semaphore("po")))

    B = Builder(nc)
    qT, qV, qS, qD, qG = (B.q[k] for k in ("tensor", "vector", "scalar", "sync", "gpsimd"))

    nv2 = nbuf.rearrange("p (b k t) -> p b k t", b=2, k=KC)
    av = acc.rearrange("p (k t) -> p k t", k=KC)
    bv = bbuf.rearrange("p (k t) -> p k t", k=KC)
    hv = hb16.rearrange("p (k t) -> p k t", k=KC)
    xv = xt.rearrange("p (k t) -> p k t", k=KC)
    wg_v = wgbuf.rearrange("p (b k n) -> p b k n", b=2, k=KC)
    wu_v = wubuf.rearrange("p (b k n) -> p b k n", b=2, k=KC)
    wd_v = wdbuf.rearrange("p (b k n) -> p b k n", b=3, k=IC)
    hs_v = hslots.rearrange("p (s k n) -> p s k n", s=2, k=IC)
    wr_v = wrbuf.rearrange("p (b k n) -> p b k n", b=2, k=KC)
    wi_v = wibuf.rearrange("p (b k n) -> p b k n", b=2, k=KC)
    wn_v = wnbuf.rearrange("p (b k n) -> p b k n", b=2, k=KC)
    wo_v = wobuf.rearrange("p (m k n) -> p m k n", m=KC, k=KC)
    cex2 = cex.rearrange("q (b t) -> q b t", b=2)

    HZ = {
        "xt_ve": 0, "xt_sx": 0,
        "sq_pe": {},
        "gev_ve": [0, 0],
        "csb_ve": [0, 0],
        "cst_pe": [0, 0],
        "cex_sx": [0, 0],
        "route_ve": 0,
        "n_pe": [0, 0],
        "e_cex": [0, 0],
        "n_ve": [0, 0],
        "igs_ve": 0, "ups_ve": 0,
        "hb_pe": 0,
        "ms_pe": 0,
        "row_po": 0,
        "row_ve": 0,
        "e_cb": 0,
        "gs_ve": [0, 0, 0, 0],
        "wgbuf_pe": {}, "wdbuf_pe": {},
        "hslot_pe": {},
        "ln_banks": (0, 1),
        "bf16w": False,
        "e_cast": {},
        "xb_pe": 0,
    }

    e_const = 0
    for t, h in [(ident, id_h), (iota, io_h), (binT, bin_h), (brT, br_h),
                 (biT, bi_h), (boT, bo_h), (laT, la_h), (maskc, mk_h)]:
        e_const = B.dma("sync", t, h, sx)
    e_gwT = B.dma("gpsimd", gwT.rearrange("p (k e) -> p k e", k=KC),
                  gw_h.rearrange("(k p) e -> p k e", p=128), gq)
    qV.op(lambda e: e.memset(onesc, 1.0))
    qV.op(lambda e: e.memset(onesr, 1.0))
    qV.op(lambda e: e.memset(onesc_b, 1.0))
    qV.op(lambda e: e.memset(onesr_b, 1.0))
    e_hc0 = B.vop(lambda e: e.memset(hcar, 0.0), writes=("hcar",), sem=ve)
    qS.wait(sx, e_const)
    e_avec = B.vop(lambda e: e.activation(avec, laT, AF.Sigmoid),
                   writes=("avec",), sem=se, eng="scalar")

    def layer_norm(src_dram, t0, T, par, ready_evt):
        nv = nv2[:, par]
        for so, w in _subs(T):
            ln_load(src_dram, t0, so, w, ready_evt)
            ln_pm(par, so, w)
            ln_bc(par, nv, so, w)

    def ln_load(src_dram, t0, so, w, ready_evt):
        HZ["ln_ld"] = B.dma(
            "sync", xv[:, :, 0:w],
            src_dram[:, t0 + so:t0 + so + w].rearrange("(k p) t -> p k t", p=128),
            sx, waits=[(ve, HZ["xt_ve"]), (sx, max(HZ["xt_sx"], ready_evt))])
        HZ["ln_xb"] = B.dma(
            "gpsimd", xb_all.rearrange("p (k t) -> p k t", k=KC)[:, :, 0:w],
            src_dram[:, t0 + so:t0 + so + w].rearrange("(k p) t -> p k t", p=128),
            gq, waits=[(pe, HZ["xb_pe"])])

    def ln_pm(par, so, w):
        if True:
            e_ld = HZ["ln_ld"]
            qT.wait(sx, e_ld)
            qS.wait(sx, e_ld)
            bA, bB = HZ["ln_banks"]
            B.bank_acquire(bA)
            B.bank_acquire(bB)
            pm = PB[bA][0:1, 0:w]
            pv = PB[bB][0:1, 0:w]
            e_xb = HZ["ln_xb"]
            qT.wait(gq, e_xb)
            xbv = xb_all.rearrange("p (k t) -> p k t", k=KC)
            B.q["gpsimd"].wait(gq, e_xb)
            for k in range(KC):
                slot = k % 2
                sq = sqs[:, slot * 512:slot * 512 + w]
                B.q["gpsimd"].wait(pe, HZ["sq_pe"].get(slot, 0))
                e_sq = B.vop(lambda e, o=sq, a=xbv[:, k, 0:w], b=xbv[:, k, 0:w]:
                             e.tensor_tensor(o, a, b, ALU.mult),
                             writes=("sqs",), sem=po, eng="gpsimd")
                qT.wait(po, e_sq)
                B.mm(pm, onesc_b, xbv[:, k, 0:w], start=(k == 0), stop=(k == KC - 1),
                     sem=pe if k == KC - 1 else None)
                ev = B.mm(pv, onesc_b, sq, start=(k == 0), stop=(k == KC - 1), sem=pe)
                HZ["sq_pe"][slot] = ev
            e_pm = pe.n
            e_pv = pe.n
            HZ["xb_pe"] = e_pm
            qS.wait(pe, e_pm)
            qS.wait(pe, HZ["ms_pe"])
            qS.wait(po, HZ["row_po"])
            qS.wait(ve, HZ["row_ve"])
            e_m = B.vop(lambda e, o=mrow[:, 0:w], i=pm:
                        e.activation(o, i, AF.Identity, scale=1.0 / D),
                        writes=("mrow",), sem=se, eng="scalar")
            qS.wait(pe, e_pv)
            e_v0 = B.vop(lambda e, o=vrow[:, 0:w], i=pv:
                         e.activation(o, i, AF.Identity, scale=1.0 / D),
                         writes=("vrow",), sem=se, eng="scalar")
            B.bank_consumed(bA, se, e_m)
            B.bank_consumed(bB, se, e_v0)
            B.vop(lambda e, o=mrow_b[:, 0:w], i=mrow[:, 0:w]:
                  e.activation(o, i, AF.Identity),
                  reads=("mrow",), writes=("mrow_b",), sem=None, eng="scalar")
            e_msq = B.vop(lambda e, o=srow[:, 0:w], i=mrow[:, 0:w]:
                          e.activation(o, i, AF.Square),
                          reads=("mrow",), writes=("srow",), sem=se, eng="scalar")
            B.q["gpsimd"].wait(se, e_msq)
            B.vop(lambda e, o=vrow[:, 0:w], a=vrow[:, 0:w], b=srow[:, 0:w]:
                  e.tensor_tensor(o, a, b, ALU.subtract),
                  writes=("vrow",), eng="gpsimd")
            e_var = B.vop(lambda e, o=vrow[:, 0:w], i=vrow[:, 0:w]:
                          e.tensor_scalar(o, i, EPS, None, ALU.add),
                          reads=("vrow",), writes=("vrow",), sem=po, eng="gpsimd")
            HZ["row_po"] = e_var
            qS.wait(po, e_var)
            e_sqr = B.vop(lambda e, o=srow[:, 0:w], i=vrow[:, 0:w]:
                          e.activation(o, i, AF.Sqrt),
                          reads=("srow",), writes=("srow",), sem=se, eng="scalar")
            qV.wait(se, e_sqr)
            e_sr = B.vop(lambda e, o=srow[:, 0:w], i=srow[:, 0:w]: e.reciprocal(o, i),
                         reads=("srow",), writes=("srow",), sem=ve)
            HZ["row_ve"] = e_sr

    def ln_bc(par, nv, so, w):
        if True:
            bA, bB = HZ["ln_banks"]
            e_sr = HZ["row_ve"]
            qT.wait(ve, e_sr)
            B.bank_acquire(bA)
            B.bank_acquire(bB)
            Mb = PB[bA][:, 0:w]
            Sb = PB[bB][:, 0:w]
            B.mm(Mb, onesr_b, mrow_b[0:1, 0:w], start=True, stop=True)
            B.mm(Sb, onesr, srow[0:1, 0:w], start=True, stop=True, sem=pe)
            e_bc = pe.n
            HZ["ms_pe"] = e_bc
            qV.wait(pe, e_bc)
            qV.wait(pe, HZ["n_pe"][par])
            for k in range(KC):
                gslot = k % 2
                g = gev[:, gslot * 512:gslot * 512 + w]
                B.vop(lambda e, o=g, a=xv[:, k, 0:w], b=Mb:
                      e.tensor_tensor(o, a, b, ALU.subtract),
                      writes=("gev",))
                ev = B.vop(lambda e, o=nv[:, k, so:so + w], a=g, b=Sb:
                           e.tensor_tensor(o, a, b, ALU.mult),
                           reads=("gev",), writes=("nbuf",), sem=ve)
                HZ["gev_ve"][gslot] = ev
            HZ["xt_ve"] = ev
            B.bank_consumed(bA, ve, ev)
            B.bank_consumed(bB, ve, ev)
        HZ["n_ve"][par] = ev
        return ev

    def ln_stages(src_dram, t0, T, par, ready_key):
        nv = nv2[:, par]
        subs = _subs(T)
        out = [lambda: ln_load(src_dram, t0, subs[0][0], subs[0][1],
                               store_evts.get(ready_key, 0))]
        for i, (so, w) in enumerate(subs):
            out.append(lambda so=so, w=w: ln_pm(par, so, w))

            def stage_bc(i=i, so=so, w=w):
                ln_bc(par, nv, so, w)
                if i + 1 < len(subs):
                    so2, w2 = subs[i + 1]
                    ln_load(src_dram, t0, so2, w2, store_evts.get(ready_key, 0))
            out.append(stage_bc)
        return out

    def routing(T, par):
        nch = T // 128
        nv = nv2[:, par]
        qT.wait(ve, HZ["n_ve"][par])
        qT.wait(gq, e_gwT)
        bA, bB = HZ["ln_banks"]
        B.bank_acquire(bA)
        PL = PB[bA]
        gw3 = gwT.rearrange("p (k e) -> p k e", k=KC)
        for c in range(nch):
            for k in range(KC):
                B.mm(PL[:, c * E:(c + 1) * E], nv[:, k, c * 128:(c + 1) * 128],
                     gw3[:, k], start=(k == 0), stop=(k == KC - 1),
                     sem=pe if (k == KC - 1 and c == nch - 1) else None)
        e_pl = pe.n
        qS.wait(pe, e_pl)
        qS.wait(ve, HZ["route_ve"])
        e_sig = B.vop(lambda e, o=sall[:, 0:nch * E], i=PL[:, 0:nch * E]:
                      e.activation(o, i, AF.Sigmoid),
                      writes=("sall",), sem=se, eng="scalar")
        B.bank_consumed(bA, se, e_sig)
        qV.wait(se, e_sig)
        s4 = sall.rearrange("p (c g m) -> p c g m", c=CHMAX, g=NG)[:, 0:nch]
        gs3 = r_gs.rearrange("p (c g) -> p c g", c=CHMAX)[:, 0:nch]
        gm3 = r_gm.rearrange("p (c g) -> p c g", c=CHMAX)[:, 0:nch]
        mx2 = r_mx.rearrange("p (c o) -> p c o", c=CHMAX)[:, 0:nch]
        ws2 = r_ws.rearrange("p (c o) -> p c o", c=CHMAX)[:, 0:nch]
        cb3 = r_cb.rearrange("p (c e) -> p c e", c=CHMAX)[:, 0:nch]
        ms3 = r_ms.rearrange("p (c e) -> p c e", c=CHMAX)[:, 0:nch]
        ms4 = r_ms.rearrange("p (c g m) -> p c g m", c=CHMAX, g=NG)[:, 0:nch]
        io8 = iota[:, 0:NG].unsqueeze(1).broadcast_to([128, nch, NG])
        io32 = iota.unsqueeze(1).broadcast_to([128, nch, E])
        qV.wait(sx, e_const)
        B.vop(lambda e, o=gs3, i=s4: e.tensor_reduce(o, i, AX.X, ALU.add),
              reads=("sall",), writes=("r_gs",))
        B.vop(lambda e, o=gm3: e.memset(o, 0.0), writes=("r_gm",))

        def extract(src3, io_b, width, accum):
            B.vop(lambda e, o=mx2, i=src3: e.tensor_reduce(o, i, AX.X, ALU.max),
                  reads=("r_gs", "r_ms"), writes=("r_mx",))
            eq3 = r_t1.rearrange("p (c e) -> p c e", c=CHMAX)[:, 0:nch, 0:width]
            B.vop(lambda e, o=eq3, a=src3, b=mx2.broadcast_to([128, nch, width]):
                  e.tensor_tensor(o, a, b, ALU.is_ge),
                  reads=("r_mx",), writes=("r_t1",))
            si3 = r_t2.rearrange("p (c e) -> p c e", c=CHMAX)[:, 0:nch, 0:width]
            B.vop(lambda e, o=si3, a=eq3, b=io_b:
                  e.scalar_tensor_tensor(o, a, -BIGI, b, ALU.mult, ALU.add),
                  reads=("r_t1",), writes=("r_t2",))
            B.vop(lambda e, o=si3, i=si3: e.tensor_scalar(o, i, BIGI, None, ALU.add),
                  reads=("r_t2",), writes=("r_t2",))
            B.vop(lambda e, o=mx2, i=si3: e.tensor_reduce(o, i, AX.X, ALU.min),
                  reads=("r_t2",), writes=("r_mx",))
            fs3 = r_t1.rearrange("p (c e) -> p c e", c=CHMAX)[:, 0:nch, 0:width]
            B.vop(lambda e, o=fs3, a=io_b, b=mx2.broadcast_to([128, nch, width]):
                  e.tensor_tensor(o, a, b, ALU.is_equal),
                  reads=("r_mx",), writes=("r_t1",))
            accum(fs3)
            B.vop(lambda e, o=src3, a=fs3, b=src3:
                  e.scalar_tensor_tensor(o, a, -BIGI, b, ALU.mult, ALU.add),
                  reads=("r_t1",), writes=("r_gs", "r_ms"))

        for _ in range(TOPG):
            extract(gs3, io8, NG,
                    lambda fs3: B.vop(lambda e, o=gm3, a=gm3, b=fs3:
                                      e.tensor_tensor(o, a, b, ALU.add),
                                      reads=("r_t1", "r_gm"), writes=("r_gm",)))
        B.vop(lambda e, o=ms4, a=s4,
              b=gm3.unsqueeze(3).broadcast_to([128, nch, NG, EPG]):
              e.tensor_tensor(o, a, b, ALU.mult),
              reads=("r_gm", "sall"), writes=("r_ms",))
        B.vop(lambda e, o=cb3: e.memset(o, 0.0), writes=("r_cb",))

        def acc8(fs3):
            B.vop(lambda e, o=r_t2[:, 0:nch * E], a=r_t1[:, 0:nch * E],
                  b=r_ms[:, 0:nch * E]: e.tensor_tensor(o, a, b, ALU.mult),
                  reads=("r_t1", "r_ms"), writes=("r_t2",))
            B.vop(lambda e, o=cb3, a=cb3,
                  b=r_t2.rearrange("p (c e) -> p c e", c=CHMAX)[:, 0:nch]:
                  e.tensor_tensor(o, a, b, ALU.add),
                  reads=("r_t2", "r_cb"), writes=("r_cb",))

        for _ in range(TOPK):
            extract(ms3, io32, E, acc8)
        B.vop(lambda e, o=ws2, i=cb3: e.tensor_reduce(o, i, AX.X, ALU.add),
              reads=("r_cb",), writes=("r_ws",))
        B.vop(lambda e, o=r_ws[:, 0:nch], i=r_ws[:, 0:nch]:
              e.tensor_scalar(o, i, 1e-12, None, ALU.max),
              reads=("r_ws",), writes=("r_ws",))
        B.vop(lambda e, o=r_ws[:, 0:nch], i=r_ws[:, 0:nch]: e.reciprocal(o, i),
              reads=("r_ws",), writes=("r_ws",))
        e_cb = B.vop(lambda e, o=cb3, a=cb3, b=ws2.broadcast_to([128, nch, E]):
                     e.tensor_tensor(o, a, b, ALU.mult),
                     reads=("r_ws", "r_cb"), writes=("r_cb",), sem=ve)
        HZ["e_cb"] = e_cb
        return e_cb

    def routing_finish(T, par):
        nch = T // 128
        qT.wait(ve, HZ["e_cb"])
        qT.wait(sx, e_const)
        bA, bB = HZ["ln_banks"]
        B.bank_acquire(bA)
        B.bank_acquire(bB)
        for c in range(nch):
            PT = PB[[bA, bB][c // 4]][0:E, (c % 4) * 128:(c % 4 + 1) * 128]
            last = (c == nch - 1)
            B.q["tensor"].op(lambda e, o=PT,
                             i=r_cb.rearrange("p (c e) -> p c e", c=CHMAX)[:, c],
                             d=ident: e.transpose(o, i, d).then_inc(pe.h, 1))
            pe.bump()
        qV.wait(pe, pe.n)
        qV.wait(sx, HZ["cex_sx"][par])
        e_tr = 0
        for b in range((nch + 3) // 4):
            wb = min(4, nch - b * 4) * 128
            e_tr = B.vop(lambda e, o=cex2[:, par, b * 512:b * 512 + wb],
                         i=PB[[bA, bB][b]][0:E, 0:wb]:
                         e.tensor_copy(o, i), writes=("cex",), sem=ve)
            B.bank_consumed([bA, bB][b], ve, e_tr)
        HZ["route_ve"] = e_tr
        HZ["e_cex"][par] = e_tr
        return e_tr

    store_evts = {}

    class OnceStage:
        def __init__(self, f):
            self.f, self.done = f, False

        def __call__(self):
            if not self.done:
                self.done = True
                self.f()

    def mk_stages(kind, src, t0, T, par, ready_key):
        st_ = ln_stages(src, t0, T, par, ready_key)
        if kind == "lnr":
            st_.append(lambda: routing(T, par))
            st_.append(lambda: routing_finish(T, par))
        return [OnceStage(f) for f in st_]

    def load_expert(ex, e_wld):
        bgu = ex % 2
        bd = ex % 3
        B.dma("gpsimd", wg_v[:, bgu], wg_h[ex].rearrange("(k p) n -> p k n", p=128),
              gq, waits=[(pe, HZ["wgbuf_pe"].get(bgu, 0))])
        B.dma("gpsimd", wu_v[:, bgu], wu_h[ex].rearrange("(k p) n -> p k n", p=128), gq)
        e_wld[ex] = B.dma("gpsimd", wd_v[:, bd],
                          wd_h[ex].rearrange("(k p) n -> p k n", p=128),
                          gq, waits=[(pe, HZ["wdbuf_pe"].get(bd, 0))])

    def down_m(ex, s_i, slot, subs, av_, m):
        so, w = subs[s_i]
        bd = ex % 3
        bk = YBANKS[m % 3]
        B.bank_acquire(bk)
        yB = PB[bk]
        for ic in range(IC):
            B.mm(yB[:, 0:w], wd_v[:, bd, ic, m * 128:(m + 1) * 128],
                 hs_v[:, slot, ic, 0:w],
                 start=(ic == 0), stop=(ic == IC - 1),
                 sem=pe if ic == IC - 1 else None)
        last_pe = pe.n
        qV.wait(pe, last_pe)
        ev = B.vop(lambda e, o=av_[:, m, so:so + w], a=av_[:, m, so:so + w],
                   b=yB[:, 0:w]: e.tensor_tensor(o, a, b, ALU.add),
                   reads=("acc",), writes=("acc",), sem=ve)
        B.bank_consumed(bk, ve, ev)
        return last_pe

    def down_expert(ex, s_i, slot, subs, av_, e_hwr, e_down, e_down_end,
                    ms=range(KC)):
        qT.wait(ve, e_hwr[(ex, s_i)])
        last_pe = None
        for m in ms:
            last_pe = down_m(ex, s_i, slot, subs, av_, m)
        e_down[(ex, s_i)] = last_pe
        HZ["hslot_pe"][slot] = last_pe
        if s_i == len(subs) - 1:
            e_down_end[ex] = last_pe
            HZ["wdbuf_pe"][ex % 3] = last_pe

    def ffn_tile(t0, T, par, src_dram, dst_dram, next_stages=(), own_stages=()):
        HZ["ln_banks"] = (0, 1)
        for s in own_stages:
            s()
        subs = _subs(T)
        nv = nv2[:, par]
        e_cex = HZ["e_cex"][par]
        e_f0 = B.vop(lambda e, o=acc[:, :]: e.memset(o, 0.0),
                     writes=("acc",), sem=ve)
        e_wld, e_hwr, e_down, e_cstv = {}, {}, {}, {}
        e_down_end = {}

        def load_cst(ex):
            if ex >= E:
                return
            le = ex % 2
            e_cstv[ex] = B.dma(
                "sync", cst[0:1, le * TMAX:le * TMAX + T],
                cex2[ex:ex + 1, par, 0:T].squeeze(1) if False else cex[ex:ex + 1, par * TMAX:par * TMAX + T],
                sx, waits=[(ve, e_cex), (pe, HZ["cst_pe"][le])])
            HZ["cex_sx"][par] = e_cstv[ex]

        load_expert(0, e_wld)
        load_expert(1, e_wld)
        load_cst(0)
        load_cst(1)
        items = [(ex, s_i) for ex in range(NEXP) for s_i in range(len(subs))]
        for idx, (ex, s_i) in enumerate(items):
            so, w = subs[s_i]
            le = ex % 2
            bgu = ex % 2
            use_c = ex < E
            if use_c:
                qT.wait(sx, e_cstv[ex])
                B.bank_acquire(6)
                CB = PB[6][:, 0:w]
                B.mm(CB, onesr, cst[0:1, le * TMAX + so:le * TMAX + so + w],
                     start=True, stop=True, sem=pe)
                e_cbank = pe.n
                HZ["cst_pe"][le] = e_cbank
                qS.wait(pe, e_cbank)
                qS.wait(ve, HZ["csb_ve"][le])
                e_cready = B.vop(
                    lambda e, o=c_sb[:, le * 512:le * 512 + w], i=CB:
                    e.copy(o, i), writes=("c_sb",), sem=se, eng="scalar")
                B.bank_consumed(6, se, e_cready)
            qT.wait(gq, e_wld[ex])
            qT.wait(ve, HZ["n_ve"][par])
            if idx > 0:
                pex, ps = items[idx - 1]
            for ic in range(IC):
                gbk, ubk = ic % 2, 2 + ic % 2
                B.bank_acquire(gbk)
                B.bank_acquire(ubk)
                gB, uB = PB[gbk], PB[ubk]
                for k in range(KC):
                    B.mm(gB[:, 0:w], wg_v[:, bgu, k, ic * 128:(ic + 1) * 128],
                         nv[:, k, so:so + w], start=(k == 0),
                         stop=(k == KC - 1), sem=pe if k == KC - 1 else None)
                e_g = pe.n
                for k in range(KC):
                    B.mm(uB[:, 0:w], wu_v[:, bgu, k, ic * 128:(ic + 1) * 128],
                         nv[:, k, so:so + w], start=(k == 0),
                         stop=(k == KC - 1), sem=pe if k == KC - 1 else None)
                e_u = pe.n
                qS.wait(pe, e_g)
                qS.wait(ve, HZ["gs_ve"][ic])
                e_gs = B.vop(lambda e, o=gs_scr[:, ic * 512:ic * 512 + w],
                             i=gB[:, 0:w]: e.activation(o, i, AF.Silu),
                             writes=("gs_scr",), sem=se, eng="scalar")
                B.bank_consumed(gbk, se, e_gs)
                qV.wait(pe, e_u)
                if use_c:
                    qV.wait(se, e_cready)
                    e_uc = B.vop(lambda e, o=uc_scr[:, ic * 512:ic * 512 + w],
                                 a=uB[:, 0:w], b=c_sb[:, le * 512:le * 512 + w]:
                                 e.tensor_tensor(o, a, b, ALU.mult),
                                 writes=("uc_scr",), sem=ve)
                    HZ["csb_ve"][le] = e_uc
                else:
                    e_uc = B.vop(lambda e, o=uc_scr[:, ic * 512:ic * 512 + w],
                                 i=uB[:, 0:w]: e.tensor_copy(o, i),
                                 writes=("uc_scr",), sem=ve)
                B.bank_consumed(ubk, ve, e_uc)
                if idx > 0:
                    qT.wait(ve, e_hwr[(pex, ps)])
                    down_m(pex, ps, (idx - 1) % 2, subs, av, 2 * ic)
                    down_m(pex, ps, (idx - 1) % 2, subs, av, 2 * ic + 1)
            qV.wait(se, se.n)
            qV.wait(pe, HZ["hslot_pe"].get(idx % 2, 0))
            qV.wait(ve, e_f0)
            for ic in range(IC):
                ev = B.vop(lambda e, o=hs_v[:, idx % 2, ic, 0:w],
                           a=gs_scr[:, ic * 512:ic * 512 + w],
                           b=uc_scr[:, ic * 512:ic * 512 + w]:
                           e.tensor_tensor(o, a, b, ALU.mult),
                           reads=("gs_scr", "uc_scr"), writes=("hslots",),
                           sem=ve)
                HZ["gs_ve"][ic] = ev
            e_hwr[(ex, s_i)] = ev
            if idx > 0:
                e_down[(pex, ps)] = pe.n
                HZ["hslot_pe"][(idx - 1) % 2] = pe.n
                if ps == len(subs) - 1:
                    e_down_end[pex] = pe.n
                    HZ["wdbuf_pe"][pex % 3] = pe.n
            if s_i == len(subs) - 1:
                HZ["wgbuf_pe"][ex % 2] = pe.n
                if not HZ["bf16w"] and ex not in HZ["e_cast"]:
                    B.dma("gpsimd", wgs_h[ex], wg_h[ex], gq)
                    B.dma("gpsimd", wus_h[ex], wu_h[ex], gq)
                    HZ["e_cast"][ex] = B.dma("gpsimd", wds_h[ex], wd_h[ex], gq)
                if ex + 2 < NEXP:
                    load_expert(ex + 2, e_wld)
                if ex + 2 < E:
                    load_cst(ex + 2)
                if ex >= 6 and (ex - 6) % 3 == 0:
                    k = (ex - 6) // 3
                    if k < len(next_stages):
                        next_stages[k]()
        pex, ps = items[-1]
        down_expert(pex, ps, (len(items) - 1) % 2, subs, av, e_hwr, e_down,
                    e_down_end)
        HZ["n_pe"][par] = pe.n
        e_acc_done = ve.n
        last_st = 0
        for s_i, (so, w) in enumerate(subs):
            e_ld = B.dma("sync", xv[:, :, 0:w],
                         src_dram[:, t0 + so:t0 + so + w].rearrange("(k p) t -> p k t", p=128),
                         sx, waits=[(ve, HZ["xt_ve"]), (sx, HZ["xt_sx"])])
            qV.wait(sx, e_ld)
            qV.wait(ve, e_acc_done)
            for k in range(KC):
                ev = B.vop(lambda e, o=xv[:, k, 0:w], a=av[:, k, so:so + w],
                           b=xv[:, k, 0:w]:
                           e.scalar_tensor_tensor(o, a, 0.5, b, ALU.mult, ALU.add),
                           reads=("acc",), writes=("xt",), sem=ve)
            HZ["xt_ve"] = ev
            last_st = B.dma(
                "sync", dst_dram[:, t0 + so:t0 + so + w].rearrange("(k p) t -> p k t", p=128),
                xv[:, :, 0:w], sx, waits=[(ve, ev)])
            HZ["xt_sx"] = last_st
        return last_st

    def griffin_tile(t0, T, par, t_i, e_wld_g, e_pm, e_wo, next_stages=(),
                     own_stages=()):
        HZ["ln_banks"] = (4, 5)
        for s in own_stages:
            s()
        subs = _subs(T)
        nv = nv2[:, par]

        def load_gw(m):
            buf = m % 2
            B.dma("gpsimd", wr_v[:, buf],
                  wr_h[:, m * 128:(m + 1) * 128].rearrange("(k p) j -> p k j", p=128),
                  gq, waits=[(pe, e_pm.get((t_i, m - 2), 0) if m >= 2
                              else e_pm.get((t_i - 1, KC - 2 + m), 0))])
            B.dma("gpsimd", wi_v[:, buf],
                  wi_h[:, m * 128:(m + 1) * 128].rearrange("(k p) j -> p k j", p=128), gq)
            e_wld_g[(t_i, m)] = B.dma(
                "gpsimd", wn_v[:, buf],
                win_h[:, m * 128:(m + 1) * 128].rearrange("(k p) j -> p k j", p=128), gq)

        load_gw(0)
        load_gw(1)
        for m in range(KC):
            buf = m % 2
            qT.wait(gq, e_wld_g[(t_i, m)])
            qT.wait(ve, HZ["n_ve"][par])
            for s_i, (so, w) in enumerate(subs):
                j = (m * len(subs) + s_i) % 2
                rbk, ibk, pbk = [0, 3][j], [1, 6][j], [2, 7][j]
                B.bank_acquire(rbk)
                RB = PB[rbk]
                for k in range(KC):
                    B.mm(RB[:, 0:w], wr_v[:, buf, k], nv[:, k, so:so + w],
                         start=(k == 0), stop=(k == KC - 1),
                         sem=pe if k == KC - 1 else None)
                e_r = pe.n
                B.bank_acquire(ibk)
                IB = PB[ibk]
                for k in range(KC):
                    B.mm(IB[:, 0:w], wi_v[:, buf, k], nv[:, k, so:so + w],
                         start=(k == 0), stop=(k == KC - 1),
                         sem=pe if k == KC - 1 else None)
                e_i = pe.n
                B.bank_acquire(pbk)
                PP = PB[pbk]
                for k in range(KC):
                    B.mm(PP[:, 0:w], wn_v[:, buf, k], nv[:, k, so:so + w],
                         start=(k == 0), stop=(k == KC - 1),
                         sem=pe if k == KC - 1 else None)
                e_p = pe.n
                e_pm[(t_i, m)] = e_p
                gslot = 0
                qS.wait(pe, e_r)
                qS.wait(se, e_avec)
                qS.wait(ve, HZ["gev_ve"][gslot])
                e_sg = B.vop(lambda e, o=gev[:, 0:512][:, 0:w], i=RB[:, 0:w],
                             b=brT[:, m:m + 1]: e.activation(o, i, AF.Sigmoid, bias=b),
                             writes=("gev",), sem=se, eng="scalar")
                B.bank_consumed(rbk, se, e_sg)
                qV.wait(se, e_sg)
                e_a = B.vop(lambda e, o=av[:, m, so:so + w], i=gev[:, 0:512][:, 0:w],
                            s=avec[:, m:m + 1]: e.tensor_scalar(o, i, s, None, ALU.mult),
                            writes=("acc",), sem=ve)
                HZ["gev_ve"][gslot] = e_a
                qS.wait(pe, e_i)
                qS.wait(ve, HZ["igs_ve"])
                e_ig = B.vop(lambda e, o=igs[:, 0:w], i=IB[:, 0:w], b=biT[:, m:m + 1]:
                             e.activation(o, i, AF.Sigmoid, bias=b),
                             writes=("igs",), sem=se, eng="scalar")
                B.bank_consumed(ibk, se, e_ig)
                qS.wait(pe, e_p)
                qS.wait(ve, HZ["ups_ve"])
                e_p1 = B.vop(lambda e, o=ups[:, 0:w], i=PP[:, 0:w], b=binT[:, m:m + 1]:
                             e.activation(o, i, AF.Identity, bias=b),
                             writes=("ups",), sem=se, eng="scalar")
                B.bank_consumed(pbk, se, e_p1)
                qV.wait(se, e_p1)
                e_um = B.vop(lambda e, o=ups[:, 0:w], a=igs[:, 0:w], b=ups[:, 0:w]:
                             e.tensor_tensor(o, a, b, ALU.mult),
                             writes=("ups",), sem=ve)
                HZ["igs_ve"] = e_um
                ev = B.vop(lambda e, o=bv[:, m, so:so + w], a=av[:, m, so:so + w],
                           b=ups[:, 0:w]:
                           e.scalar_tensor_tensor(o, a, -1.0, b, ALU.add, ALU.mult),
                           reads=("ups", "acc"), writes=("bbuf",), sem=ve)
                HZ["ups_ve"] = ev
            if t_i == 0:
                qV.wait(sx, e_const)
                B.vop(lambda e, o=av[:, m, 0:WARM], i=av[:, m, 0:WARM],
                      s=maskc[:, 0:1]: e.tensor_scalar(o, i, s, None, ALU.mult),
                      reads=("acc",), writes=("acc",))
                B.vop(lambda e, o=bv[:, m, 0:WARM], i=bv[:, m, 0:WARM],
                      s=maskc[:, 0:1]: e.tensor_scalar(o, i, s, None, ALU.mult),
                      reads=("bbuf",), writes=("bbuf",))
            qV.wait(ve, e_hc0)
            qV.wait(pe, HZ["hb_pe"])
            B.vop(lambda e, o=hscr[:, 0:T], a=av[:, m, 0:T], b=bv[:, m, 0:T],
                  i=hcar[:, m:m + 1]:
                  e.tensor_tensor_scan(o, a, b, i, ALU.mult, ALU.add),
                  reads=("acc", "bbuf", "hcar"), writes=("hscr",))
            B.vop(lambda e, o=hcar[:, m:m + 1], i=hscr[:, T - 1:T]: e.tensor_copy(o, i),
                  reads=("hscr",), writes=("hcar",))
            e_h = B.vop(lambda e, o=hv[:, m, 0:T], i=hscr[:, 0:T]: e.tensor_copy(o, i),
                        reads=("hscr",), writes=("hb16",), sem=ve)
            e_pm[(t_i, m, "h")] = e_h
            if m + 2 < KC:
                load_gw(m + 2)
            if 1 <= m <= 5:
                k = m - 1
                if k < len(next_stages):
                    next_stages[k]()
        HZ["n_pe"][par] = pe.n
        qT.wait(gq, e_wo)
        for k in range(KC):
            qT.wait(ve, e_pm[(t_i, k, "h")])
        last_pe_h = 0
        last_st = 0
        for s_i, (so, w) in enumerate(subs):
            e_x1 = B.dma("sync", xv[:, :, 0:w],
                         x1_h[:, t0 + so:t0 + so + w].rearrange("(k p) t -> p k t", p=128),
                         sx, waits=[(ve, HZ["xt_ve"]), (sx, HZ["xt_sx"])])
            ev = 0
            for m2 in range(KC):
                bk = 2 + (m2 % 2)
                B.bank_acquire(bk)
                GB = PB[bk]
                for k in range(KC):
                    B.mm(GB[:, 0:w], wo_v[:, m2, k], hv[:, k, so:so + w],
                         start=(k == 0), stop=(k == KC - 1),
                         sem=pe if k == KC - 1 else None)
                e_g = pe.n
                last_pe_h = e_g
                gslot = m2 % 2
                qS.wait(pe, e_g)
                qS.wait(ve, HZ["gev_ve"][gslot])
                e_gv = B.vop(lambda e, o=gev[:, gslot * 512:gslot * 512 + w],
                             i=GB[:, 0:w], b=boT[:, m2:m2 + 1]:
                             e.activation(o, i, AF.Identity, bias=b, scale=-1.0),
                             writes=("gev",), sem=se, eng="scalar")
                B.bank_consumed(bk, se, e_gv)
                qV.wait(se, e_gv)
                qV.wait(sx, e_x1)
                ev = B.vop(lambda e, o=xv[:, m2, 0:w], a=xv[:, m2, 0:w],
                           b=gev[:, gslot * 512:gslot * 512 + w]:
                           e.tensor_tensor(o, a, b, ALU.add),
                           writes=("xt",), sem=ve)
                HZ["gev_ve"][gslot] = ev
            HZ["xt_ve"] = ev
            lo = max(so, WARM) if t0 == 0 else so
            g0 = t0 + lo - WARM
            ww = so + w - lo
            if ww > 0:
                last_st = B.dma(
                    "sync", x2_h[:, g0:g0 + ww].rearrange("(k p) t -> p k t", p=128),
                    xv[:, :, lo - so:lo - so + ww], sx, waits=[(ve, ev)])
                HZ["xt_sx"] = last_st
            k = 5 + s_i
            if k < len(next_stages):
                next_stages[k]()
        HZ["hb_pe"] = last_pe_h
        return last_st

    starts_A = np.cumsum([0] + TILES_A[:-1]).tolist()
    starts_B = np.cumsum([0] + TILES_B[:-1]).tolist()
    starts_C = np.cumsum([0] + TILES_C[:-1]).tolist()
    n_jobs = len(TILES_A) + len(TILES_B) + len(TILES_C)
    pars = [j % 2 for j in range(n_jobs)]
    stages = []
    ji = 0
    for t_i, (t0, T) in enumerate(zip(starts_A, TILES_A)):
        stages.append(mk_stages("lnr", x_h, t0, T, pars[ji], None))
        ji += 1
    for t_i, (t0, T) in enumerate(zip(starts_B, TILES_B)):
        stages.append(mk_stages("ln", x1_h, t0, T, pars[ji], ("A", t_i)))
        ji += 1
    for t_i, (t0, T) in enumerate(zip(starts_C, TILES_C)):
        stages.append(mk_stages("lnr", x2_h, t0, T, pars[ji],
                                ("B", min(t_i + 1, len(TILES_B) - 1))))
        ji += 1

    for s in stages[0]:
        s()
    for t_i, (t0, T) in enumerate(zip(starts_A, TILES_A)):
        par = pars[t_i]
        nxt = stages[t_i + 1] if t_i + 1 < len(stages) else ()
        store_evts[("A", t_i)] = ffn_tile(t0, T, par, x_h, x1_h, nxt,
                                          stages[t_i])
        if t_i == 0:
            HZ["bf16w"] = True
    A_end_pe = pe.n
    qG.wait(pe, pe.n)
    qG.wait(ve, ve.n)
    qG.wait(se, se.n)
    qV.wait(pe, A_end_pe)
    qS.wait(pe, A_end_pe)
    e_wo = 0
    for m2 in range(KC):
        e_wo = B.dma("gpsimd", wo_v[:, m2],
                     wo_h[:, m2 * 128:(m2 + 1) * 128].rearrange("(k p) j -> p k j", p=128),
                     gq)
    e_wld_g, e_pm_g = {}, {}
    for t_i, (t0, T) in enumerate(zip(starts_B, TILES_B)):
        ji = len(TILES_A) + t_i
        par = pars[ji]
        nxt = stages[ji + 1] if ji + 1 < len(stages) else ()
        store_evts[("B", t_i)] = griffin_tile(t0, T, par, t_i, e_wld_g, e_pm_g,
                                              e_wo, nxt, stages[ji])
    B_end_pe = pe.n
    qG.wait(pe, pe.n)
    qG.wait(ve, ve.n)
    qG.wait(se, se.n)
    qV.wait(pe, B_end_pe)
    qS.wait(pe, B_end_pe)
    HZ["wgbuf_pe"] = {}
    HZ["wdbuf_pe"] = {}
    HZ["hslot_pe"] = {}
    for t_i, (t0, T) in enumerate(zip(starts_C, TILES_C)):
        ji = len(TILES_A) + len(TILES_B) + t_i
        par = pars[ji]
        nxt = stages[ji + 1] if ji + 1 < len(stages) else ()
        ffn_tile(t0, T, par, x2_h, out_h, nxt, stages[ji])
    B.q["sync"].op(lambda e, v=sx.n: e.wait_ge(sx.h, v))

    with st:
        with nc.Block() as block:
            @block.sync
            def _(eng):
                for f in B.q["sync"].ops:
                    f(eng)

            @block.gpsimd
            def _(eng):
                for f in B.q["gpsimd"].ops:
                    f(eng)

            @block.tensor
            def _(eng):
                for f in B.q["tensor"].ops:
                    f(eng)

            @block.vector
            def _(eng):
                for f in B.q["vector"].ops:
                    f(eng)

            @block.scalar
            def _(eng):
                for f in B.q["scalar"].ops:
                    f(eng)
    return nc


def kernel(L0_attn_x, L0_attn_W_in, L0_attn_b_in, L0_attn_W_r, L0_attn_b_r,
           L0_attn_W_i, L0_attn_b_i, L0_attn_W_out, L0_attn_b_out,
           L0_attn_log_a, L0_ffn_gate_w, L0_ffn_expert_gate,
           L0_ffn_expert_up, L0_ffn_expert_down, L0_ffn_shared_gate_w,
           L0_ffn_shared_up_w, L0_ffn_shared_down_w):
    f32 = np.float32
    x = np.asarray(L0_attn_x, f32)
    xT = np.ascontiguousarray(np.swapaxes(x, 1, 2))

    def lay(v):
        return np.ascontiguousarray(np.asarray(v, f32))

    wg = np.ascontiguousarray(np.concatenate(
        [np.swapaxes(lay(L0_ffn_expert_gate), 1, 2),
         lay(L0_ffn_shared_gate_w).T[None]], 0))
    wu = np.ascontiguousarray(np.concatenate(
        [np.swapaxes(lay(L0_ffn_expert_up), 1, 2),
         lay(L0_ffn_shared_up_w).T[None]], 0))
    wd = np.ascontiguousarray(np.concatenate(
        [np.swapaxes(lay(L0_ffn_expert_down), 1, 2),
         lay(L0_ffn_shared_down_w).T[None]], 0))
    gw = np.ascontiguousarray(lay(L0_ffn_gate_w).T)

    def vec_pk(v):
        return np.ascontiguousarray(lay(v).reshape(KC, 128).T)

    base = {
        "wg": wg, "wu": wu, "wd": wd, "gw": gw,
        "win": lay(L0_attn_W_in), "wr": lay(L0_attn_W_r),
        "wi": lay(L0_attn_W_i), "wo": lay(L0_attn_W_out),
        "bin": vec_pk(L0_attn_b_in), "br": vec_pk(L0_attn_b_r),
        "bi": vec_pk(L0_attn_b_i), "bo": vec_pk(L0_attn_b_out),
        "la": vec_pk(L0_attn_log_a),
        "iota": np.tile(np.arange(E, dtype=f32), (128, 1)),
        "ident": np.eye(128, dtype=f32),
    }
    nc = build_program()
    in_maps = []
    for c in range(8):
        b, half = c // 2, c % 2
        if half == 0:
            xs = np.concatenate([xT[b][:, 0:WARM], xT[b][:, 0:OWN]], axis=1)
            mk = np.zeros((128, 1), f32)
        else:
            xs = xT[b][:, OWN - WARM:2 * OWN]
            mk = np.ones((128, 1), f32)
        m = dict(base)
        m["x"] = np.ascontiguousarray(xs)
        m["mask"] = mk
        in_maps.append(m)

    res = run_bass_kernel_spmd(nc, in_maps, core_ids=list(range(8)))
    out = np.empty((4, 1024, 8192), f32)
    for c in range(8):
        b, half = c // 2, c % 2
        out[b][:, half * OWN:(half + 1) * OWN] = res.results[c]["out"]
    return np.ascontiguousarray(np.swapaxes(out, 1, 2))
